# version 22
# baseline (speedup 1.0000x reference)
import numpy as np
import ml_dtypes
from contextlib import ExitStack
import concourse.bass as bass
import concourse.mybir as mybir
from concourse.bass_utils import run_bass_kernel_spmd

F32 = mybir.dt.float32
BF16 = mybir.dt.bfloat16
I32 = mybir.dt.int32
U16 = mybir.dt.uint16
A = mybir.AluOpType
AF = mybir.ActivationFunctionType
AX = mybir.AxisListType

R = 8
B_, S, D = 4, 2048, 2048
T = B_ * S
TKC = T // R
FT = 256
ISQ = 0.08838834764831845
NQS = 4


def build_nc():
    nc = bass.Bass("TRN2", target_bir_lowering=False)

    def din(name, shape, dt):
        return nc.dram_tensor(name, shape, dt, kind="ExternalInput").ap()

    def dint(name, shape, dt):
        return nc.dram_tensor(name, shape, dt, kind="Internal").ap()

    x_tok = din("x_tok", [TKC, D], F32)
    xT_s = din("xT_s", [FT, T], F32)
    w_qkv_c = [din(f"w{p}_c", [D, FT], F32) for p in "qkv"]
    w_qkv_se = [din(f"w{p}_se", [D, FT], F32) for p in "qkv"]
    wo_c = din("wo_c", [FT, D], F32)
    wo_se = din("wo_se", [FT, D], F32)
    bqkv_h = din("bqkv", [128, 6], F32)
    bo_h = din("bo_s", [128, 2], F32)
    ffn_p = {w: din(f"{w}_p", [2048, 1024], I32) for w in ("g", "u", "d")}
    ffn_s = {w: din(f"{w}_s", [2048, 64], F32) for w in ("g", "u", "d")}
    ident_h = din("ident", [128, 128], BF16)
    yT_s = nc.dram_tensor("yT_s", [FT, T], F32, kind="ExternalOutput").ap()

    ag_in = [dint(f"ag_in{h}", [D, 512], BF16) for h in range(2)]
    ag_out = [dint(f"ag_out{h}", [R * D, 512], BF16) for h in range(2)]
    dec = {w: dint(f"dec_{w}", [16, 128, 2048], BF16) for w in ("g", "u", "d")}
    dec_a = dint("dec_a", [8, 128, 2048], BF16)
    part = [[dint(f"part{c}h{h}", [D // 2, S], F32) for h in range(2)] for c in range(4)]
    rs_o = [[dint(f"rs_o{c}h{h}", [128, S], F32) for h in range(2)] for c in range(4)]

    with ExitStack() as ctx:
        def sbuf(name, shape, dt):
            return ctx.enter_context(nc.sbuf_tensor(name, shape, dt))

        def psum(name, shape, dt):
            return ctx.enter_context(nc.psum_tensor(name, shape, dt))

        xnt = sbuf("xnt", [128, 32768], BF16)
        uni = sbuf("uni", [128, 32768], BF16)
        ao = sbuf("ao", [128, 4096], BF16)
        wpool = sbuf("wpool", [128, 8192], BF16)
        apb = sbuf("apb", [128, 4096], BF16)
        pk = sbuf("pk", [128, 2048], I32)
        dtmp = sbuf("dtmp", [128, 4096], I32)
        codes = sbuf("codes", [128, 2048], U16)
        dq = sbuf("dq", [128, 2048], BF16)
        stage = sbuf("stage", [128, 4096], BF16)
        scb = sbuf("scb", [128, 128], F32)
        id_sb = sbuf("id_sb", [128, 128], BF16)
        bqkv_sb = sbuf("bqkv_sb", [128, 6], F32)
        bo_sb = sbuf("bo_sb", [128, 2], F32)
        gg = sbuf("gg", [128, 1024], BF16)
        yp = sbuf("yp", [128, 1024], F32)
        smal = sbuf("smal", [128, 16], F32)
        xn_t = None

        P1 = psum("P1", [128, 2048], F32)
        P2 = [psum("P2a", [128, 512], F32), psum("P2b", [128, 512], F32)]
        P3 = [psum("P3a", [128, 1024], BF16), psum("P3b", [128, 1024], BF16)]

        qsems = {q: [f"{q}{i}" for i in range(NQS)] for q in ("sync", "scalar", "gpsimd")}
        semnames = ["cc", "pe", "dve", "act"] + sum(qsems.values(), [])
        sem = {n: ctx.enter_context(nc.semaphore(n)) for n in semnames}
        cnt = {n: 0 for n in semnames}
        qrot = {q: 0 for q in qsems}
        ops = {"sync": [], "vector": [], "scalar": [], "tensor": [], "gpsimd": []}

        def em(eng, fn):
            ops[eng].append(fn)

        def W(eng, s, v):
            if v > 0:
                em(eng, lambda e, _s=sem[s], _v=v: e.wait_ge(_s, _v))

        def Wl(eng, toks):
            for (s, v) in toks:
                W(eng, s, v)

        def inc(s, n=1):
            cnt[s] += n
            return cnt[s]

        def DMA(q, out_ap, in_ap, waits=()):
            Wl(q, waits)
            sname = qsems[q][qrot[q] % NQS]
            qrot[q] += 1
            c = inc(sname, 16)
            em(q, lambda e, o=out_ap, i=in_ap, sm=sem[sname]:
               e.dma_start(out=o, in_=i).then_inc(sm, 16))
            return (sname, c)

        def qmark(q):
            return [(s, cnt[s]) for s in qsems[q] if cnt[s] > 0]

        def OP(eng, build, incs=None, waits=()):
            Wl(eng, waits)
            if incs:
                c = inc(incs)
                em(eng, lambda e, b=build, sm=sem[incs]: b(e).then_inc(sm, 1))
                return (incs, c)
            em(eng, lambda e, b=build: b(e))
            return None

        u_f32 = uni[:].bitcast(F32)
        wp_f32 = wpool[:].bitcast(F32)
        jnk = codes[:].bitcast(BF16)
        fnb = dtmp[:].bitcast(F32)

        xn_t = stage[:, 0:2048]
        U_QT, U_KT, U_V, U_P, U_PN, U_VT, U_PT = 0, 4096, 8192, 12288, 14336, 16384, 16384
        Z = ("pe", 0)

        p1f = [[Z], [Z], [Z], [Z]]
        p2f = [Z, Z]
        p3f = [Z, Z]

        for o, i in ((id_sb, ident_h), (bqkv_sb, bqkv_h), (bo_sb, bo_h)):
            DMA("sync", o[:], i)
        con_ld = qmark("sync")

        OP("vector", lambda v: v.memset(smal[:, 10:11], 1e-5))
        OP("vector", lambda v: v.memset(smal[:, 11:12], 16128.0), incs="dve")
        ln_done = []
        ag_done = [None, None]
        pe_tr = Z
        for tt in range(8):
            slot = tt % 2
            xsl = wp_f32[:, slot * 2048:(slot + 1) * 2048]
            ldw = [ln_done[tt - 2]] if tt >= 2 else ()
            ldv = DMA("sync", xsl, x_tok[tt * 128:(tt + 1) * 128, :], waits=ldw)
            rs1 = OP("vector", lambda v, x=xsl: v.reduce_sum(smal[:, 0:1], x, axis=AX.X),
                     incs="dve", waits=[ldv])
            nmn = OP("vector", lambda v: v.tensor_single_scalar(smal[:, 2:3], smal[:, 0:1],
                     -1.0 / D, A.mult), incs="dve", waits=[rs1])
            sq = OP("scalar", lambda s, x=xsl: s.activation(out=jnk[:], in_=x, func=AF.Square,
                    accum_out=smal[:, 1:2]), incs="act", waits=[ldv])
            OP("vector", lambda v: v.tensor_single_scalar(smal[:, 3:4], smal[:, 1:2],
                                                          1.0 / D, A.mult), waits=[sq, nmn])
            m2c = OP("vector", lambda v: v.tensor_tensor(smal[:, 4:5], smal[:, 2:3],
                     smal[:, 2:3], A.mult), incs="dve")
            vd = OP("vector", lambda v: v.tensor_tensor(smal[:, 12:13], smal[:, 3:4],
                    smal[:, 4:5], A.subtract), incs="dve", waits=[m2c])
            sdc = OP("scalar", lambda s: s.activation(out=smal[:, 13:14], in_=smal[:, 12:13],
                     func=AF.Sqrt, bias=smal[:, 10:11], scale=1.0), incs="act", waits=[vd])
            rcc = OP("vector", lambda v: v.reciprocal(smal[:, 5:6], smal[:, 13:14]),
                     incs="dve", waits=[sdc])
            xw = [rcc] + ([pe_tr] if tt >= 1 else [])
            xnd = OP("vector", lambda v, x=xsl: v.tensor_scalar(xn_t[:], x, smal[:, 2:3],
                     smal[:, 5:6], A.add, A.mult), incs="dve", waits=xw)
            ln_done.append(xnd)
            Wl("tensor", [xnd])
            if tt == 0:
                Wl("tensor", con_ld)
            for half in range(2):
                pd = P3[half]
                Wl("tensor", [p3f[half]])
                for k in range(8):
                    dtl = half * 8 + k
                    pe_tr = OP("tensor", lambda t, p=pd[:, k * 128:(k + 1) * 128],
                               i=xn_t[:, dtl * 128:(dtl + 1) * 128]:
                               t.transpose(p, i, id_sb[:]),
                               incs="pe" if k == 7 else None) or pe_tr
                dst = uni[:].rearrange("p (dt q) -> p dt q", q=1024)[
                    :, half * 8:half * 8 + 8, tt * 128:(tt + 1) * 128]
                ev = OP("scalar", lambda s, o=dst,
                        i=pd[:].rearrange("p (dt q) -> p dt q", q=128):
                        s.copy(out=o, in_=i), incs="act", waits=[pe_tr])
                p3f[half] = ev
            if tt in (3, 7):
                th = tt // 4
                W("gpsimd", "act", cnt["act"])
                for dtl in range(16):
                    DMA("gpsimd", ag_in[th][dtl * 128:(dtl + 1) * 128, :],
                        uni[:, dtl * 1024 + th * 512:dtl * 1024 + th * 512 + 512])
                Wl("gpsimd", qmark("gpsimd"))
                ag_done[th] = OP("gpsimd", lambda g, _h=th: g.collective_compute(
                    "AllGather", A.bypass, replica_groups=[list(range(R))],
                    ins=[ag_in[_h].opt()], outs=[ag_out[_h].opt()]), incs="cc")
        ag_st = qmark("gpsimd")

        st_free2 = [("act", 0), ("act", 0), ("act", 0), ("act", 0)]
        a1_srcs = [(w_qkv_c[0], w_qkv_se[0]), (w_qkv_c[1], w_qkv_se[1]),
                   (w_qkv_c[2], w_qkv_se[2]), (wo_c, wo_se)]
        a1_ttp = [None] * 4
        deca_st = {}
        a1_state = {"sidx": 0}

        def emit_A1(pi):
            sidx = a1_state["sidx"]
            ch, sh = a1_srcs[pi]
            base = (pi % 2) * 8192
            lw = list(ag_st) if pi < 2 else [a1_ttp[pi - 2]]
            toks = []
            if pi < 3:
                cin = ch[:].rearrange("(dt p) f -> p dt f", p=128)
                sin = sh[:].rearrange("(dt p) f -> p dt f", p=128)
                toks.append(DMA("scalar",
                                u_f32[:, base:base + 4096].rearrange("p (dt f) -> p dt f", f=FT),
                                cin, waits=lw))
                toks.append(DMA("scalar",
                                u_f32[:, base + 4096:base + 8192].rearrange("p (dt f) -> p dt f", f=FT),
                                sin))
            else:
                toks.append(DMA("scalar", u_f32[:, base:base + 2048], ch[0:128, :], waits=lw))
                toks.append(DMA("scalar", u_f32[:, base + 2048:base + 4096], ch[128:256, :]))
                toks.append(DMA("scalar", u_f32[:, base + 4096:base + 6144], sh[0:128, :]))
                toks.append(DMA("scalar", u_f32[:, base + 6144:base + 8192], sh[128:256, :]))
            Wl("vector", toks)
            a1_tt = None
            for f in range(2):
                for hg in range(2):
                    ss = sidx % 4
                    sidx += 1
                    Wl("vector", [st_free2[ss]])
                    for k in range(8):
                        dtl = hg * 8 + k
                        if pi < 3:
                            a_ = u_f32[:, base + dtl * 256 + f * 128:base + dtl * 256 + f * 128 + 128]
                            b_ = u_f32[:, base + 4096 + dtl * 256 + f * 128:base + 4096 + dtl * 256 + f * 128 + 128]
                        else:
                            a_ = u_f32[:, base + f * 2048 + dtl * 128:base + f * 2048 + (dtl + 1) * 128]
                            b_ = u_f32[:, base + 4096 + f * 2048 + dtl * 128:base + 4096 + f * 2048 + (dtl + 1) * 128]
                        o = stage[:, ss * 1024 + k * 128:ss * 1024 + (k + 1) * 128]
                        a1_tt = OP("vector", lambda v, _o=o, _a=a_, _b=b_:
                                   v.tensor_tensor(_o, _a, _b, A.mult),
                                   incs="dve" if k == 7 else None) or a1_tt
                    stv = DMA("scalar", dec_a[2 * pi + f][:, hg * 1024:(hg + 1) * 1024],
                              stage[:, ss * 1024:(ss + 1) * 1024], waits=[a1_tt])
                    st_free2[ss] = stv
                    deca_st.setdefault(2 * pi + f, []).append(stv)
            a1_ttp[pi] = a1_tt
            a1_state["sidx"] = sidx

        Tt = [dtmp[:, i * 1024:(i + 1) * 1024] for i in range(4)]
        pk_free = [("dve", 0), ("dve", 0)]
        sc_free = [("dve", 0), ("dve", 0)]
        st_free = st_free2
        dec_store = {}
        dec_ptr = {}
        dec_i = [0]
        dec_pend = []
        dec_order = [(w, rt) for rt in range(16) for w in ("g", "u")] + \
                    [("d", rt) for rt in range(16)]
        p0_act = cnt["act"]

        def emit_decode_dve(n):
            for _ in range(n):
                if dec_i[0] >= len(dec_order):
                    return
                i = dec_i[0]
                dec_i[0] += 1
                w, rt = dec_order[i]
                slot = i % 2
                pks = pk[:, slot * 1024:(slot + 1) * 1024]
                scs = scb[:, slot * 64:(slot + 1) * 64]
                pkv = DMA("gpsimd", pks, ffn_p[w][rt * 128:(rt + 1) * 128, :],
                          waits=[pk_free[slot]])
                scv = DMA("gpsimd", scs, ffn_s[w][rt * 128:(rt + 1) * 128, :],
                          waits=[sc_free[slot]])
                w0 = [pkv] + ([("act", p0_act)] if i == 0 else [])
                dcd = None
                for lohi in range(2):
                    if lohi == 0:
                        OP("vector", lambda v, p=pks: v.tensor_single_scalar(
                           Tt[0][:], p, 7, A.bitwise_and), waits=w0)
                        OP("vector", lambda v, p=pks: v.tensor_single_scalar(
                           Tt[1][:], p, 8, A.bitwise_and))
                    else:
                        OP("vector", lambda v, p=pks: v.tensor_scalar(
                           Tt[0][:], p, 4, 7, A.logical_shift_right, A.bitwise_and))
                        pkfree = OP("vector", lambda v, p=pks: v.tensor_scalar(
                                    Tt[1][:], p, 4, 8, A.logical_shift_right, A.bitwise_and),
                                    incs="dve")
                        pk_free[slot] = pkfree
                    bbv = OP("vector", lambda v: v.tensor_scalar(Tt[2][:], Tt[0][:],
                             64, 16128, A.mult, A.add), incs="dve")
                    mmn = OP("vector", lambda v: v.scalar_tensor_tensor(Tt[3][:], Tt[0][:],
                             16128.0, Tt[2][:], A.mult, A.min), incs="dve", waits=[bbv])
                    dst = codes[:, lohi:2048:2]
                    dcd = OP("vector", lambda v, _d=dst: v.scalar_tensor_tensor(
                             _d, Tt[1][:], 4096.0, Tt[3][:], A.mult, A.add),
                             incs="dve", waits=[mmn])
                dqd = OP("vector", lambda v, _s=scs:
                         v.tensor_tensor(dq[:].rearrange("p (j r) -> p j r", r=32),
                                         codes[:].bitcast(BF16).rearrange("p (j r) -> p j r", r=32),
                                         _s.unsqueeze(2).broadcast_to([128, 64, 32]),
                                         A.mult), incs="dve",
                         waits=[scv, dcd, dec_ptr.get(i - 1, Z)])
                sc_free[slot] = dqd
                dec_pend.append((i, w, rt, dqd))

        def emit_decode_pe():
            i, w, rt, dqd = dec_pend.pop(0)
            Wl("tensor", [dqd])
            ptr = None
            for half in range(2):
                pd = P3[half]
                Wl("tensor", [p3f[half]])
                for k in range(8):
                    kt2 = half * 8 + k
                    ptr = OP("tensor", lambda t, p=pd[:, k * 128:(k + 1) * 128],
                             i2=dq[:, kt2 * 128:(kt2 + 1) * 128]:
                             t.transpose(p, i2, id_sb[:]),
                             incs="pe" if k == 7 else None) or ptr
                ss2 = (2 * i + half) % 4
                ev = OP("scalar", lambda s, o=stage[:, ss2 * 1024:(ss2 + 1) * 1024],
                        i2=pd[:]: s.copy(out=o, in_=i2), incs="act",
                        waits=[ptr, st_free[ss2]])
                p3f[half] = ev
                stv = DMA("scalar", dec[w][rt][:, half * 1024:(half + 1) * 1024],
                          stage[:, ss2 * 1024:(ss2 + 1) * 1024], waits=[ev])
                st_free[ss2] = stv
            dec_ptr[i] = ptr
            dec_store[(w, rt)] = stv

        def emit_decode(n):
            for _ in range(n):
                if dec_i[0] >= len(dec_order):
                    return
                emit_decode_dve(1)
                emit_decode_pe()

        xnt_h0 = [None] * 4
        xnt_all = [None] * 4
        apb_use = [Z, Z]
        wslot_use = [Z, Z, Z, Z]
        yp_free = [("act", 0), ("act", 0)]
        b5_last = [0, 0, 0, 0]
        rs_done = [0, 0, 0, 0]
        fn_free = [("dve", 0), ("dve", 0)]
        gg_free = [("dve", 0), ("dve", 0)]
        qk_ev = [Z]
        ao_ev = [Z]
        b5_intd = [0]

        def emit_xnt_load(c):
            w0 = [("pe", b5_last[c - 1])] if c > 0 else []
            for th in range(2):
                waits = w0 + ([("cc", ag_done[th][1])] if c == 0 else [])
                w0 = []
                for dtl in range(16):
                    for h2 in range(2):
                        DMA("sync",
                            xnt[:, dtl * 2048 + h2 * 1024 + th * 512:
                                dtl * 2048 + h2 * 1024 + th * 512 + 512],
                            ag_out[th][(2 * c + h2) * D + dtl * 128:
                                       (2 * c + h2) * D + (dtl + 1) * 128, :],
                            waits=waits)
                        waits = ()
                if th == 0:
                    xnt_h0[c] = qmark("sync")
            xnt_all[c] = qmark("sync")

        def emit_B2(c):
            last_ev = Z
            for ft in range(6):
                slot = ft % 2
                apb_ld = DMA("sync", apb[:, slot * 2048:(slot + 1) * 2048],
                             dec_a[ft], waits=deca_st[ft] + [apb_use[slot]])
                for tg in (0, 2, 1, 3):
                    g = (4 * ft + tg) % 2
                    pp = P2[g]
                    Wl("tensor", [apb_ld] + (xnt_h0[c] if tg in (0, 2) else xnt_all[c]))
                    Wl("tensor", [p2f[g]])
                    cp = None
                    for dtl in range(16):
                        cp = OP("tensor", lambda t, _p=pp[:], _w=apb[:, slot * 2048 + dtl * 128:
                                slot * 2048 + (dtl + 1) * 128],
                                _x=xnt[:, dtl * 2048 + tg * 512:dtl * 2048 + (tg + 1) * 512],
                                st=(dtl == 0), sp=(dtl == 15): t.matmul(_p, _w, _x, start=st, stop=sp),
                                incs="pe" if dtl == 15 else None) or cp
                    if tg == 3:
                        apb_use[slot] = cp
                    if ft < 2:
                        dst = uni[:, U_QT + ft * 2048 + tg * 512:U_QT + ft * 2048 + (tg + 1) * 512]
                    elif ft < 4:
                        dst = uni[:, U_KT + (ft - 2) * 2048 + tg * 512:U_KT + (ft - 2) * 2048 + (tg + 1) * 512]
                    else:
                        dst = uni[:, U_VT + (ft - 4) * 2048 + tg * 512:U_VT + (ft - 4) * 2048 + (tg + 1) * 512]
                    ev = OP("scalar", lambda s, o=dst, i2=pp[:], bi=bqkv_sb[:, ft:ft + 1]:
                            s.activation(out=o, in_=i2, func=AF.Identity, bias=bi),
                            incs="act", waits=[cp])
                    p2f[g] = ev
                    last_ev = ev
            qk_ev[0] = last_ev

        def emit_B3(c):
            Wl("tensor", [qk_ev[0]])
            for h in range(2):
                for half in range(2):
                    pd = P3[half]
                    Wl("tensor", [p3f[half]])
                    ptr = None
                    for k in range(8):
                        tk = half * 8 + k
                        ptr = OP("tensor", lambda t, p=pd[:, k * 128:(k + 1) * 128],
                                 i2=uni[:, U_VT + h * 2048 + tk * 128:U_VT + h * 2048 + (tk + 1) * 128]:
                                 t.transpose(p, i2, id_sb[:]),
                                 incs="pe" if k == 7 else None) or ptr
                    dstv = uni[:, U_V + half * 2048:U_V + (half + 1) * 2048].rearrange(
                        "p (a r) -> p a r", r=256)[:, :, h * 128:(h + 1) * 128]
                    ev = OP("scalar", lambda s, o=dstv,
                            i2=pd[:].rearrange("p (a r) -> p a r", r=128):
                            s.copy(out=o, in_=i2), incs="act", waits=[ptr])
                    p3f[half] = ev

        def emit_B4(c, h, avd):
            prev_nm = Z
            ptev = Z
            prev_ea = prev_eb = Z
            for q in range(16):
                if q == 0:
                    Wl("tensor", [qk_ev[0]])
                    for k in range(4):
                        Wl("tensor", p1f[k])
                sc = [None, None]
                for kt in range(4):
                    if q > 0 and kt in (0, 2):
                        Wl("tensor", [prev_ea if kt == 0 else prev_eb] +
                           ([prev_nm] if kt == 0 else []))
                    r_ = OP("tensor", lambda t, _p=P1[:, kt * 512:(kt + 1) * 512],
                            _q=uni[:, U_QT + h * 2048 + q * 128:U_QT + h * 2048 + (q + 1) * 128],
                            _k=uni[:, U_KT + h * 2048 + kt * 512:U_KT + h * 2048 + (kt + 1) * 512]:
                            t.matmul(_p, _q, _k, start=True, stop=True),
                            incs="pe" if kt in (1, 3) else None)
                    if kt == 1:
                        sc[0] = r_
                    elif kt == 3:
                        sc[1] = r_
                rmx = OP("vector", lambda v: v.reduce_max(smal[:, 6:7], P1[:, 0:512], axis=AX.X,
                         negate=True), incs="dve", waits=[sc[0], prev_ea])
                nm = OP("vector", lambda v: v.tensor_single_scalar(smal[:, 7:8], smal[:, 6:7],
                        ISQ, A.mult), incs="dve", waits=[rmx])
                ea = OP("scalar", lambda s: s.activation(out=uni[:, U_P:U_P + 1024],
                        in_=P1[:, 0:1024], func=AF.Exp, scale=ISQ, bias=smal[:, 7:8],
                        accum_out=smal[:, 8:9]), incs="act", waits=[sc[0], nm])
                eb = OP("scalar", lambda s: s.activation(out=uni[:, U_P + 1024:U_P + 2048],
                        in_=P1[:, 1024:2048], func=AF.Exp, scale=ISQ, bias=smal[:, 7:8],
                        accum_out=smal[:, 14:15]), incs="act", waits=[sc[1]])
                sad = OP("vector", lambda v: v.tensor_tensor(smal[:, 9:10], smal[:, 8:9],
                         smal[:, 14:15], A.add), incs="dve", waits=[eb])
                rcp = OP("vector", lambda v: v.reciprocal(smal[:, 9:10], smal[:, 9:10]),
                         incs="dve", waits=[sad])
                pn = OP("vector", lambda v: v.tensor_single_scalar(uni[:, U_PN:U_PN + 2048],
                        uni[:, U_P:U_P + 2048], smal[:, 9:10], A.mult), incs="dve",
                        waits=[rcp])
                prev_ea, prev_eb, prev_nm = ea, eb, nm
                ql = q % 4
                Wl("tensor", [pn])
                for half in range(2):
                    pd = P3[half]
                    Wl("tensor", [p3f[half]])
                    ptr = None
                    for k in range(8):
                        kt = half * 8 + k
                        ptr = OP("tensor", lambda t, p=pd[:, k * 128:(k + 1) * 128],
                                 i2=uni[:, U_PN + kt * 128:U_PN + (kt + 1) * 128]:
                                 t.transpose(p, i2, id_sb[:]),
                                 incs="pe" if k == 7 else None) or ptr
                    dstv = uni[:, U_PT + half * 8 * 512:U_PT + (half * 8 + 8) * 512].rearrange(
                        "p (a r) -> p a r", r=512)[:, :, ql * 128:(ql + 1) * 128]
                    ev = OP("scalar", lambda s, o=dstv,
                            i2=pd[:].rearrange("p (a r) -> p a r", r=128):
                            s.copy(out=o, in_=i2), incs="act",
                            waits=[ptr] + ([avd] if half == 0 else []))
                    p3f[half] = ev
                    ptev = ev
                if ql == 3:
                    g4 = q // 4
                    gs = g4 % 2
                    Wl("tensor", [ptev, p2f[gs]])
                    for kt in range(16):
                        r_ = OP("tensor", lambda t, _p=P2[gs][:],
                                _v=uni[:, U_V + kt * 256 + h * 128:U_V + kt * 256 + (h + 1) * 128],
                                _pt=uni[:, U_PT + kt * 512:U_PT + (kt + 1) * 512],
                                st=(kt == 0), sp=(kt == 15): t.matmul(_p, _v, _pt, start=st, stop=sp),
                                incs="pe" if kt == 15 else None)
                        avd = r_ or avd
                    ev = OP("scalar", lambda s, o=ao[:, h * 2048 + g4 * 512:h * 2048 + (g4 + 1) * 512],
                            i2=P2[gs][:]: s.copy(out=o, in_=i2), incs="act", waits=[avd])
                    p2f[gs] = ev
                    ao_ev[0] = ev
            p1f[0] = [prev_ea, prev_nm]
            p1f[1] = [prev_ea]
            p1f[2] = [prev_eb]
            p1f[3] = [prev_eb]
            return avd

        def emit_B5(c, avd, dec_per_j=0):
            first = True
            for j in range(16):
                if dec_per_j:
                    emit_decode(dec_per_j)
                ldg = DMA("sync", wpool[:, (j % 2) * 2048:(j % 2 + 1) * 2048], dec["g"][j],
                          waits=[dec_store[("g", j)], wslot_use[j % 2]])
                ldu = DMA("sync", wpool[:, (2 + j % 2) * 2048:(3 + j % 2) * 2048], dec["u"][j],
                          waits=[dec_store[("u", j)], wslot_use[2 + j % 2]])
                for tg in range(4):
                    qb = tg % 2
                    bg = P1[:, (2 * qb) * 512:(2 * qb + 1) * 512]
                    bu = P1[:, (2 * qb + 1) * 512:(2 * qb + 2) * 512]
                    Wl("tensor", [ldg, ldu] + xnt_all[c])
                    Wl("tensor", p1f[2 * qb] + p1f[2 * qb + 1])
                    gmm = umm = None
                    for kd in range(16):
                        gmm = OP("tensor", lambda t, _p=bg, _w=wpool[:, (j % 2) * 2048 + kd * 128:
                                 (j % 2) * 2048 + (kd + 1) * 128],
                                 _x=xnt[:, kd * 2048 + tg * 512:kd * 2048 + (tg + 1) * 512],
                                 st=(kd == 0), sp=(kd == 15): t.matmul(_p, _w, _x, start=st, stop=sp),
                                 incs="pe" if kd == 15 else None) or gmm
                    for kd in range(16):
                        umm = OP("tensor", lambda t, _p=bu, _w=wpool[:, (2 + j % 2) * 2048 + kd * 128:
                                 (2 + j % 2) * 2048 + (kd + 1) * 128],
                                 _x=xnt[:, kd * 2048 + tg * 512:kd * 2048 + (tg + 1) * 512],
                                 st=(kd == 0), sp=(kd == 15): t.matmul(_p, _w, _x, start=st, stop=sp),
                                 incs="pe" if kd == 15 else None) or umm
                    if tg == 3:
                        wslot_use[j % 2] = gmm
                        wslot_use[2 + j % 2] = umm
                    geld = OP("scalar", lambda s, _g=bg, o=gg[:, qb * 512:(qb + 1) * 512]:
                              s.activation(out=o, in_=_g, func=AF.Gelu_apprx_tanh),
                              incs="act", waits=[gmm, gg_free[qb]])
                    p1f[2 * qb] = [geld]
                    iw = [umm, geld]
                    if first:
                        iw += [avd, ao_ev[0]]
                        first = False
                    intd = OP("vector", lambda v, _u=bu,
                              o=uni[:, j * 2048 + tg * 512:j * 2048 + (tg + 1) * 512],
                              g2=gg[:, qb * 512:(qb + 1) * 512]:
                              v.tensor_tensor(o, g2, _u, A.mult), incs="dve", waits=iw)
                    p1f[2 * qb + 1] = [intd]
                    gg_free[qb] = intd
                while dec_pend:
                    emit_decode_pe()
            b5_last[c] = cnt["pe"]
            b5_intd[0] = cnt["dve"]

        def emit_B6(c):
            W("tensor", "act", ao_ev[0][1])
            W("tensor", "dve", b5_intd[0])
            if c == 0:
                W("tensor", "dve", a1_all)
            wo_toks = []
            for kt in range(2):
                wo_toks.append(DMA("gpsimd", wpool[:, (2 + kt) * 2048:(3 + kt) * 2048],
                                   dec_a[6 + kt], waits=deca_st[6 + kt] + [wslot_use[2 + kt]]))
            part_last = None
            rs_half = [None, None]
            for idx, dtl in enumerate(list(range(0, 16, 2)) + list(range(1, 16, 2))):
                slot = idx % 2
                ldd = DMA("gpsimd", wpool[:, slot * 2048:(slot + 1) * 2048],
                          dec["d"][dtl],
                          waits=[dec_store[("d", dtl)], wslot_use[slot]])
                for tg in range(4):
                    gs = (4 * dtl + tg) % 2
                    pp = P2[gs]
                    Wl("tensor", [ldd] + wo_toks)
                    Wl("tensor", [p2f[gs]])
                    for kt in range(2):
                        OP("tensor", lambda t, _p=pp[:],
                           _w=wpool[:, (2 + kt) * 2048 + dtl * 128:(2 + kt) * 2048 + (dtl + 1) * 128],
                           _a=ao[:, kt * 2048 + tg * 512:kt * 2048 + (tg + 1) * 512],
                           st=(kt == 0): t.matmul(_p, _w, _a, start=st, stop=False))
                    pdn = None
                    for kn in range(16):
                        pdn = OP("tensor", lambda t, _p=pp[:],
                                 _w=wpool[:, slot * 2048 + kn * 128:slot * 2048 + (kn + 1) * 128],
                                 _i=uni[:, kn * 2048 + tg * 512:kn * 2048 + (tg + 1) * 512],
                                 sp=(kn == 15): t.matmul(_p, _w, _i, start=False, stop=sp),
                                 incs="pe" if kn == 15 else None) or pdn
                    if tg == 3:
                        wslot_use[slot] = pdn
                        if dtl == 15:
                            wslot_use[2] = pdn
                            wslot_use[3] = pdn
                    ypc = OP("scalar", lambda s, o=yp[:, gs * 512:(gs + 1) * 512], i2=pp[:]:
                             s.copy(out=o, in_=i2), incs="act",
                             waits=[pdn, yp_free[gs]])
                    p2f[gs] = ypc
                    part_last = DMA("scalar",
                                    part[c][dtl % 2][(dtl // 2) * 128:(dtl // 2 + 1) * 128,
                                                     tg * 512:(tg + 1) * 512],
                                    yp[:, gs * 512:(gs + 1) * 512], waits=[ypc])
                    yp_free[gs] = part_last
                if idx in (7, 15):
                    h = idx // 8
                    Wl("gpsimd", qmark("scalar"))
                    rs_half[h] = OP("gpsimd", lambda g, _c=c, _h=h: g.collective_compute(
                        "ReduceScatter", A.add, replica_groups=[list(range(R))],
                        ins=[part[_c][_h].opt()], outs=[rs_o[_c][_h].opt()]), incs="cc")
            rs_done[c] = rs_half

        def emit_F(c):
            seen = set()
            stv = None
            for pc in range(8):
                dtl, tg = pc // 4, pc % 4
                slot = pc % 2
                rsb = fnb[:, slot * 1536:slot * 1536 + 512]
                xtb = fnb[:, slot * 1536 + 512:slot * 1536 + 1024]
                yb = fnb[:, slot * 1536 + 1024:slot * 1536 + 1536]
                firstw = [("cc", rs_done[c][dtl][1])] if dtl not in seen else []
                seen.add(dtl)
                l1 = DMA("gpsimd", rsb,
                         rs_o[c][dtl][0:128, tg * 512:(tg + 1) * 512],
                         waits=firstw + [fn_free[slot]])
                l2 = DMA("gpsimd", xtb,
                         xT_s[dtl * 128:(dtl + 1) * 128, c * 2048 + tg * 512:c * 2048 + (tg + 1) * 512])
                yd = OP("vector", lambda v, _r=rsb, _x=xtb, _y=yb, bi=bo_sb[:, dtl:dtl + 1]:
                        v.scalar_tensor_tensor(_y, _r, bi, _x, A.add, A.add),
                        incs="dve", waits=[l1, l2])
                stv = DMA("gpsimd",
                          yT_s[dtl * 128:(dtl + 1) * 128, c * 2048 + tg * 512:c * 2048 + (tg + 1) * 512],
                          yb, waits=[yd])
                fn_free[slot] = yd
            return stv

        for pi in range(4):
            emit_decode(6)
            emit_A1(pi)
        emit_decode(8)
        a1_all = cnt["dve"]
        a1_st = qmark("scalar")
        emit_xnt_load(0)
        emit_B2(0)
        emit_B3(0)
        avd = emit_B4(0, 0, Z)
        avd = emit_B4(0, 1, avd)
        emit_B5(0, avd, dec_per_j=1)
        emit_B6(0)
        for c in range(1, 4):
            emit_xnt_load(c)
            emit_B2(c)
            emit_B3(c)
            avd = emit_B4(c, 0, avd)
            avd = emit_B4(c, 1, avd)
            emit_B5(c, avd)
            emit_B6(c)
            emit_F(c - 1)
        last_st = emit_F(3)
        Wl("gpsimd", [last_st])
        Wl("gpsimd", qmark("gpsimd"))

        with nc.Block() as block:
            @block.sync
            def _(e):
                for f in ops["sync"]:
                    f(e)

            @block.vector
            def _(e):
                for f in ops["vector"]:
                    f(e)

            @block.scalar
            def _(e):
                for f in ops["scalar"]:
                    f(e)

            @block.tensor
            def _(e):
                for f in ops["tensor"]:
                    f(e)

            @block.gpsimd
            def _(e):
                for f in ops["gpsimd"]:
                    f(e)
    return nc


_CACHE = {}


def kernel(L0_attn_x, L0_attn_wq_codes, L0_attn_wq_scales, L0_attn_bq,
           L0_attn_wk_codes, L0_attn_wk_scales, L0_attn_bk,
           L0_attn_wv_codes, L0_attn_wv_scales, L0_attn_bv,
           L0_attn_wo_codes, L0_attn_wo_scales, L0_attn_bo,
           L0_ffn_gate_W_packed, L0_ffn_gate_scales,
           L0_ffn_up_W_packed, L0_ffn_up_scales,
           L0_ffn_down_W_packed, L0_ffn_down_scales):
    if "nc" not in _CACHE:
        _CACHE["nc"] = build_nc()
    nc = _CACHE["nc"]

    ac = np.ascontiguousarray
    xf = np.asarray(L0_attn_x, dtype=np.float32).reshape(T, D)
    xT = ac(xf.T)
    rep = np.repeat(np.arange(64), 32)
    wq_se_f = np.asarray(L0_attn_wq_scales)[rep, :]
    wk_se_f = np.asarray(L0_attn_wk_scales)[rep, :]
    wv_se_f = np.asarray(L0_attn_wv_scales)[rep, :]
    wo_se_f = np.asarray(L0_attn_wo_scales)[rep, :]
    ident = np.eye(128, dtype=ml_dtypes.bfloat16)
    bq = np.asarray(L0_attn_bq)
    bk = np.asarray(L0_attn_bk)
    bv = np.asarray(L0_attn_bv)
    bo = np.asarray(L0_attn_bo)

    in_maps = []
    for c in range(R):
        f0 = FT * c
        n0 = 2048 * c
        m = {
            "x_tok": ac(xf[TKC * c:TKC * (c + 1)]),
            "xT_s": ac(xT[f0:f0 + FT]),
            "wq_c": ac(np.asarray(L0_attn_wq_codes)[:, f0:f0 + FT]),
            "wk_c": ac(np.asarray(L0_attn_wk_codes)[:, f0:f0 + FT]),
            "wv_c": ac(np.asarray(L0_attn_wv_codes)[:, f0:f0 + FT]),
            "wq_se": ac(wq_se_f[:, f0:f0 + FT]),
            "wk_se": ac(wk_se_f[:, f0:f0 + FT]),
            "wv_se": ac(wv_se_f[:, f0:f0 + FT]),
            "wo_c": ac(np.asarray(L0_attn_wo_codes)[f0:f0 + FT, :]),
            "wo_se": ac(wo_se_f[f0:f0 + FT, :]),
            "bqkv": ac(np.stack([bq[f0:f0 + 128], bq[f0 + 128:f0 + 256],
                                 bk[f0:f0 + 128], bk[f0 + 128:f0 + 256],
                                 bv[f0:f0 + 128], bv[f0 + 128:f0 + 256]], axis=1)),
            "bo_s": ac(np.stack([bo[f0:f0 + 128], bo[f0 + 128:f0 + 256]], axis=1)),
            "g_p": ac(np.asarray(L0_ffn_gate_W_packed)[n0:n0 + 2048]),
            "g_s": ac(np.asarray(L0_ffn_gate_scales)[n0:n0 + 2048]),
            "u_p": ac(np.asarray(L0_ffn_up_W_packed)[n0:n0 + 2048]),
            "u_s": ac(np.asarray(L0_ffn_up_scales)[n0:n0 + 2048]),
            "d_p": ac(np.asarray(L0_ffn_down_W_packed)[:, 1024 * c:1024 * (c + 1)]),
            "d_s": ac(np.asarray(L0_ffn_down_scales)[:, 64 * c:64 * (c + 1)]),
            "ident": ident,
        }
        in_maps.append(m)

    res = run_bass_kernel_spmd(nc, in_maps, core_ids=list(range(R)))
    shards = [np.asarray(res.results[c]["yT_s"]) for c in range(R)]
    yT = np.concatenate(shards, axis=0)
    return ac(yT.T).reshape(B_, S, D)


# version 23
# speedup vs baseline: 1.0672x; 1.0672x over previous
import numpy as np
import ml_dtypes
from contextlib import ExitStack
import concourse.bass as bass
import concourse.mybir as mybir
from concourse.bass_utils import run_bass_kernel_spmd

F32 = mybir.dt.float32
BF16 = mybir.dt.bfloat16
I32 = mybir.dt.int32
U16 = mybir.dt.uint16
A = mybir.AluOpType
AF = mybir.ActivationFunctionType
AX = mybir.AxisListType

R = 8
B_, S, D = 4, 2048, 2048
T = B_ * S
TKC = T // R
FT = 256
ISQ = 0.08838834764831845
NQS = 4


def build_nc():
    nc = bass.Bass("TRN2", target_bir_lowering=False)

    def din(name, shape, dt):
        return nc.dram_tensor(name, shape, dt, kind="ExternalInput").ap()

    def dint(name, shape, dt):
        return nc.dram_tensor(name, shape, dt, kind="Internal").ap()

    x_tok = din("x_tok", [TKC, D], F32)
    xT_s = din("xT_s", [FT, T], F32)
    w_qkv_c = [din(f"w{p}_c", [D, FT], F32) for p in "qkv"]
    w_qkv_se = [din(f"w{p}_se", [D, FT], F32) for p in "qkv"]
    wo_c = din("wo_c", [FT, D], F32)
    wo_se = din("wo_se", [FT, D], F32)
    bqkv_h = din("bqkv", [128, 6], F32)
    bo_h = din("bo_s", [128, 2], F32)
    ffn_p = {w: din(f"{w}_p", [2048, 1024], I32) for w in ("g", "u", "d")}
    ffn_s = {w: din(f"{w}_s", [2048, 64], F32) for w in ("g", "u", "d")}
    ident_h = din("ident", [128, 128], BF16)
    yT_s = nc.dram_tensor("yT_s", [FT, T], F32, kind="ExternalOutput").ap()

    ag_in = [dint(f"ag_in{h}", [D, 512], BF16) for h in range(2)]
    ag_out = [dint(f"ag_out{h}", [R * D, 512], BF16) for h in range(2)]
    dec = {w: dint(f"dec_{w}", [16, 128, 2048], BF16) for w in ("g", "u", "d")}
    dec_a = dint("dec_a", [8, 128, 2048], BF16)
    part = [[dint(f"part{c}h{h}", [D // 2, S], F32) for h in range(2)] for c in range(4)]
    rs_o = [[dint(f"rs_o{c}h{h}", [128, S], F32) for h in range(2)] for c in range(4)]

    with ExitStack() as ctx:
        def sbuf(name, shape, dt):
            return ctx.enter_context(nc.sbuf_tensor(name, shape, dt))

        def psum(name, shape, dt):
            return ctx.enter_context(nc.psum_tensor(name, shape, dt))

        xnt = sbuf("xnt", [128, 32768], BF16)
        uni = sbuf("uni", [128, 32768], BF16)
        ao = sbuf("ao", [128, 4096], BF16)
        wpool = sbuf("wpool", [128, 8192], BF16)
        apb = sbuf("apb", [128, 4096], BF16)
        pk = sbuf("pk", [128, 2048], I32)
        dtmp = sbuf("dtmp", [128, 4096], I32)
        codes = sbuf("codes", [128, 2048], U16)
        dq = sbuf("dq", [128, 2048], BF16)
        stage = sbuf("stage", [128, 4096], BF16)
        scb = sbuf("scb", [128, 128], F32)
        id_sb = sbuf("id_sb", [128, 128], BF16)
        bqkv_sb = sbuf("bqkv_sb", [128, 6], F32)
        bo_sb = sbuf("bo_sb", [128, 2], F32)
        gg = sbuf("gg", [128, 1024], BF16)
        yp = sbuf("yp", [128, 1024], F32)
        smal = sbuf("smal", [128, 16], F32)
        xn_t = None

        P1 = psum("P1", [128, 2048], F32)
        P2 = [psum("P2a", [128, 512], F32), psum("P2b", [128, 512], F32)]
        P3 = [psum("P3a", [128, 1024], BF16), psum("P3b", [128, 1024], BF16)]

        qsems = {q: [f"{q}{i}" for i in range(NQS)] for q in ("sync", "scalar", "gpsimd")}
        semnames = ["cc", "pe", "dve", "act"] + sum(qsems.values(), [])
        sem = {n: ctx.enter_context(nc.semaphore(n)) for n in semnames}
        cnt = {n: 0 for n in semnames}
        qrot = {q: 0 for q in qsems}
        ops = {"sync": [], "vector": [], "scalar": [], "tensor": [], "gpsimd": []}

        def em(eng, fn):
            ops[eng].append(fn)

        def W(eng, s, v):
            if v > 0:
                em(eng, lambda e, _s=sem[s], _v=v: e.wait_ge(_s, _v))

        def Wl(eng, toks):
            for (s, v) in toks:
                W(eng, s, v)

        def inc(s, n=1):
            cnt[s] += n
            return cnt[s]

        def DMA(q, out_ap, in_ap, waits=()):
            Wl(q, waits)
            sname = qsems[q][qrot[q] % NQS]
            qrot[q] += 1
            c = inc(sname, 16)
            em(q, lambda e, o=out_ap, i=in_ap, sm=sem[sname]:
               e.dma_start(out=o, in_=i).then_inc(sm, 16))
            return (sname, c)

        def qmark(q):
            return [(s, cnt[s]) for s in qsems[q] if cnt[s] > 0]

        def OP(eng, build, incs=None, waits=()):
            Wl(eng, waits)
            if incs:
                c = inc(incs)
                em(eng, lambda e, b=build, sm=sem[incs]: b(e).then_inc(sm, 1))
                return (incs, c)
            em(eng, lambda e, b=build: b(e))
            return None

        u_f32 = uni[:].bitcast(F32)
        wp_f32 = wpool[:].bitcast(F32)
        jnk = codes[:].bitcast(BF16)
        fnb = dtmp[:].bitcast(F32)

        xn_t = stage[:, 0:2048]
        U_QT, U_KT, U_V, U_P, U_PN, U_VT, U_PT = 0, 4096, 8192, 12288, 14336, 16384, 16384
        Z = ("pe", 0)

        p1f = [[Z], [Z], [Z], [Z]]
        p2f = [Z, Z]
        p3f = [Z, Z]

        for o, i in ((id_sb, ident_h), (bqkv_sb, bqkv_h), (bo_sb, bo_h)):
            DMA("sync", o[:], i)
        con_ld = qmark("sync")

        OP("vector", lambda v: v.memset(smal[:, 10:11], 1e-5))
        OP("vector", lambda v: v.memset(smal[:, 11:12], 16128.0), incs="dve")
        ln_done = []
        ag_done = [None, None]
        pe_tr = Z
        for tt in range(8):
            slot = tt % 2
            xsl = wp_f32[:, slot * 2048:(slot + 1) * 2048]
            ldw = [ln_done[tt - 2]] if tt >= 2 else ()
            ldv = DMA("sync", xsl, x_tok[tt * 128:(tt + 1) * 128, :], waits=ldw)
            rs1 = OP("vector", lambda v, x=xsl: v.reduce_sum(smal[:, 0:1], x, axis=AX.X),
                     incs="dve", waits=[ldv])
            nmn = OP("vector", lambda v: v.tensor_single_scalar(smal[:, 2:3], smal[:, 0:1],
                     -1.0 / D, A.mult), incs="dve", waits=[rs1])
            sq = OP("scalar", lambda s, x=xsl: s.activation(out=jnk[:], in_=x, func=AF.Square,
                    accum_out=smal[:, 1:2]), incs="act", waits=[ldv])
            OP("vector", lambda v: v.tensor_single_scalar(smal[:, 3:4], smal[:, 1:2],
                                                          1.0 / D, A.mult), waits=[sq, nmn])
            m2c = OP("vector", lambda v: v.tensor_tensor(smal[:, 4:5], smal[:, 2:3],
                     smal[:, 2:3], A.mult), incs="dve")
            vd = OP("vector", lambda v: v.tensor_tensor(smal[:, 12:13], smal[:, 3:4],
                    smal[:, 4:5], A.subtract), incs="dve", waits=[m2c])
            sdc = OP("scalar", lambda s: s.activation(out=smal[:, 13:14], in_=smal[:, 12:13],
                     func=AF.Sqrt, bias=smal[:, 10:11], scale=1.0), incs="act", waits=[vd])
            rcc = OP("vector", lambda v: v.reciprocal(smal[:, 5:6], smal[:, 13:14]),
                     incs="dve", waits=[sdc])
            xw = [rcc] + ([pe_tr] if tt >= 1 else [])
            xnd = OP("vector", lambda v, x=xsl: v.tensor_scalar(xn_t[:], x, smal[:, 2:3],
                     smal[:, 5:6], A.add, A.mult), incs="dve", waits=xw)
            ln_done.append(xnd)
            Wl("tensor", [xnd])
            if tt == 0:
                Wl("tensor", con_ld)
            for half in range(2):
                pd = P3[half]
                Wl("tensor", [p3f[half]])
                for k in range(8):
                    dtl = half * 8 + k
                    pe_tr = OP("tensor", lambda t, p=pd[:, k * 128:(k + 1) * 128],
                               i=xn_t[:, dtl * 128:(dtl + 1) * 128]:
                               t.transpose(p, i, id_sb[:]),
                               incs="pe" if k == 7 else None) or pe_tr
                dst = uni[:].rearrange("p (dt q) -> p dt q", q=1024)[
                    :, half * 8:half * 8 + 8, tt * 128:(tt + 1) * 128]
                ev = OP("scalar", lambda s, o=dst,
                        i=pd[:].rearrange("p (dt q) -> p dt q", q=128):
                        s.copy(out=o, in_=i), incs="act", waits=[pe_tr])
                p3f[half] = ev
            if tt in (3, 7):
                th = tt // 4
                W("gpsimd", "act", cnt["act"])
                for dtl in range(16):
                    DMA("gpsimd", ag_in[th][dtl * 128:(dtl + 1) * 128, :],
                        uni[:, dtl * 1024 + th * 512:dtl * 1024 + th * 512 + 512])
                Wl("gpsimd", qmark("gpsimd"))
                ag_done[th] = OP("gpsimd", lambda g, _h=th: g.collective_compute(
                    "AllGather", A.bypass, replica_groups=[list(range(R))],
                    ins=[ag_in[_h].opt()], outs=[ag_out[_h].opt()]), incs="cc")
        ag_st = qmark("gpsimd")

        st_free2 = [("act", 0), ("act", 0), ("act", 0), ("act", 0)]
        a1_srcs = [(w_qkv_c[0], w_qkv_se[0]), (w_qkv_c[1], w_qkv_se[1]),
                   (w_qkv_c[2], w_qkv_se[2]), (wo_c, wo_se)]
        a1_ttp = [None] * 4
        deca_st = {}
        a1_state = {"sidx": 0}

        def emit_A1(pi):
            sidx = a1_state["sidx"]
            ch, sh = a1_srcs[pi]
            base = (pi % 2) * 8192
            lw = list(ag_st) if pi < 2 else [a1_ttp[pi - 2]]
            toks = []
            if pi < 3:
                cin = ch[:].rearrange("(dt p) f -> p dt f", p=128)
                sin = sh[:].rearrange("(dt p) f -> p dt f", p=128)
                toks.append(DMA("scalar",
                                u_f32[:, base:base + 4096].rearrange("p (dt f) -> p dt f", f=FT),
                                cin, waits=lw))
                toks.append(DMA("scalar",
                                u_f32[:, base + 4096:base + 8192].rearrange("p (dt f) -> p dt f", f=FT),
                                sin))
            else:
                toks.append(DMA("scalar", u_f32[:, base:base + 2048], ch[0:128, :], waits=lw))
                toks.append(DMA("scalar", u_f32[:, base + 2048:base + 4096], ch[128:256, :]))
                toks.append(DMA("scalar", u_f32[:, base + 4096:base + 6144], sh[0:128, :]))
                toks.append(DMA("scalar", u_f32[:, base + 6144:base + 8192], sh[128:256, :]))
            Wl("vector", toks)
            a1_tt = None
            for f in range(2):
                for hg in range(2):
                    ss = sidx % 4
                    sidx += 1
                    Wl("vector", [st_free2[ss]])
                    for k in range(8):
                        dtl = hg * 8 + k
                        if pi < 3:
                            a_ = u_f32[:, base + dtl * 256 + f * 128:base + dtl * 256 + f * 128 + 128]
                            b_ = u_f32[:, base + 4096 + dtl * 256 + f * 128:base + 4096 + dtl * 256 + f * 128 + 128]
                        else:
                            a_ = u_f32[:, base + f * 2048 + dtl * 128:base + f * 2048 + (dtl + 1) * 128]
                            b_ = u_f32[:, base + 4096 + f * 2048 + dtl * 128:base + 4096 + f * 2048 + (dtl + 1) * 128]
                        o = stage[:, ss * 1024 + k * 128:ss * 1024 + (k + 1) * 128]
                        a1_tt = OP("vector", lambda v, _o=o, _a=a_, _b=b_:
                                   v.tensor_tensor(_o, _a, _b, A.mult),
                                   incs="dve" if k == 7 else None) or a1_tt
                    stv = DMA("scalar", dec_a[2 * pi + f][:, hg * 1024:(hg + 1) * 1024],
                              stage[:, ss * 1024:(ss + 1) * 1024], waits=[a1_tt])
                    st_free2[ss] = stv
                    deca_st.setdefault(2 * pi + f, []).append(stv)
            a1_ttp[pi] = a1_tt
            a1_state["sidx"] = sidx

        Tt = [dtmp[:, i * 1024:(i + 1) * 1024] for i in range(4)]
        pk_free = [("dve", 0), ("dve", 0)]
        sc_free = [("dve", 0), ("dve", 0)]
        st_free = st_free2
        dec_store = {}
        dec_ptr = {}
        dec_i = [0]
        dec_pend = []
        dec_order = [(w, rt) for rt in range(16) for w in ("g", "u")] + \
                    [("d", rt) for rt in range(16)]
        p0_act = cnt["act"]

        def emit_decode_dve(n):
            for _ in range(n):
                if dec_i[0] >= len(dec_order):
                    return
                i = dec_i[0]
                dec_i[0] += 1
                w, rt = dec_order[i]
                slot = i % 2
                pks = pk[:, slot * 1024:(slot + 1) * 1024]
                scs = scb[:, slot * 64:(slot + 1) * 64]
                pkv = DMA("gpsimd", pks, ffn_p[w][rt * 128:(rt + 1) * 128, :],
                          waits=[pk_free[slot]])
                scv = DMA("gpsimd", scs, ffn_s[w][rt * 128:(rt + 1) * 128, :],
                          waits=[sc_free[slot]])
                w0 = [pkv] + ([("act", p0_act)] if i == 0 else [])
                dcd = None
                for lohi in range(2):
                    if lohi == 0:
                        OP("vector", lambda v, p=pks: v.tensor_single_scalar(
                           Tt[0][:], p, 7, A.bitwise_and), waits=w0)
                        OP("vector", lambda v, p=pks: v.tensor_single_scalar(
                           Tt[1][:], p, 8, A.bitwise_and))
                    else:
                        OP("vector", lambda v, p=pks: v.tensor_scalar(
                           Tt[0][:], p, 4, 7, A.logical_shift_right, A.bitwise_and))
                        pkfree = OP("vector", lambda v, p=pks: v.tensor_scalar(
                                    Tt[1][:], p, 4, 8, A.logical_shift_right, A.bitwise_and),
                                    incs="dve")
                        pk_free[slot] = pkfree
                    bbv = OP("vector", lambda v: v.tensor_scalar(Tt[2][:], Tt[0][:],
                             64, 16128, A.mult, A.add), incs="dve")
                    mmn = OP("vector", lambda v: v.scalar_tensor_tensor(Tt[3][:], Tt[0][:],
                             16128.0, Tt[2][:], A.mult, A.min), incs="dve", waits=[bbv])
                    dst = codes[:, lohi:2048:2]
                    dcd = OP("vector", lambda v, _d=dst: v.scalar_tensor_tensor(
                             _d, Tt[1][:], 4096.0, Tt[3][:], A.mult, A.add),
                             incs="dve", waits=[mmn])
                dqd = OP("vector", lambda v, _s=scs:
                         v.tensor_tensor(dq[:].rearrange("p (j r) -> p j r", r=32),
                                         codes[:].bitcast(BF16).rearrange("p (j r) -> p j r", r=32),
                                         _s.unsqueeze(2).broadcast_to([128, 64, 32]),
                                         A.mult), incs="dve",
                         waits=[scv, dcd, dec_ptr.get(i - 1, Z)])
                sc_free[slot] = dqd
                dec_pend.append((i, w, rt, dqd))

        def emit_decode_pe():
            i, w, rt, dqd = dec_pend.pop(0)
            Wl("tensor", [dqd])
            ptr = None
            for half in range(2):
                pd = P3[half]
                Wl("tensor", [p3f[half]])
                for k in range(8):
                    kt2 = half * 8 + k
                    ptr = OP("tensor", lambda t, p=pd[:, k * 128:(k + 1) * 128],
                             i2=dq[:, kt2 * 128:(kt2 + 1) * 128]:
                             t.transpose(p, i2, id_sb[:]),
                             incs="pe" if k == 7 else None) or ptr
                ss2 = (2 * i + half) % 4
                ev = OP("scalar", lambda s, o=stage[:, ss2 * 1024:(ss2 + 1) * 1024],
                        i2=pd[:]: s.copy(out=o, in_=i2), incs="act",
                        waits=[ptr, st_free[ss2]])
                p3f[half] = ev
                stv = DMA("scalar", dec[w][rt][:, half * 1024:(half + 1) * 1024],
                          stage[:, ss2 * 1024:(ss2 + 1) * 1024], waits=[ev])
                st_free[ss2] = stv
            dec_ptr[i] = ptr
            dec_store[(w, rt)] = stv

        def emit_decode(n):
            for _ in range(n):
                if dec_i[0] >= len(dec_order):
                    return
                emit_decode_dve(1)
                emit_decode_pe()

        xnt_h0 = [None] * 4
        xnt_all = [None] * 4
        apb_use = [Z, Z]
        wslot_use = [Z, Z, Z, Z]
        yp_free = [("act", 0), ("act", 0)]
        b5_last = [0, 0, 0, 0]
        rs_done = [0, 0, 0, 0]
        fn_free = [("dve", 0), ("dve", 0)]
        gg_free = [("dve", 0), ("dve", 0)]
        qk_ev = [Z]
        ao_ev = [Z]
        b5_intd = [0]

        def emit_xnt_load(c):
            w0 = [("pe", b5_last[c - 1])] if c > 0 else []
            for th in range(2):
                waits = w0 + ([("cc", ag_done[th][1])] if c == 0 else [])
                w0 = []
                for dtl in range(16):
                    for h2 in range(2):
                        DMA("sync",
                            xnt[:, dtl * 2048 + h2 * 1024 + th * 512:
                                dtl * 2048 + h2 * 1024 + th * 512 + 512],
                            ag_out[th][(2 * c + h2) * D + dtl * 128:
                                       (2 * c + h2) * D + (dtl + 1) * 128, :],
                            waits=waits)
                        waits = ()
                if th == 0:
                    xnt_h0[c] = qmark("sync")
            xnt_all[c] = qmark("sync")

        def emit_B2(c):
            last_ev = Z
            for ft in range(6):
                slot = ft % 2
                apb_ld = DMA("sync", apb[:, slot * 2048:(slot + 1) * 2048],
                             dec_a[ft], waits=deca_st[ft] + [apb_use[slot]])
                for tg in (0, 2, 1, 3):
                    g = (4 * ft + tg) % 2
                    pp = P2[g]
                    Wl("tensor", [apb_ld] + (xnt_h0[c] if tg in (0, 2) else xnt_all[c]))
                    Wl("tensor", [p2f[g]])
                    cp = None
                    for dtl in range(16):
                        cp = OP("tensor", lambda t, _p=pp[:], _w=apb[:, slot * 2048 + dtl * 128:
                                slot * 2048 + (dtl + 1) * 128],
                                _x=xnt[:, dtl * 2048 + tg * 512:dtl * 2048 + (tg + 1) * 512],
                                st=(dtl == 0), sp=(dtl == 15): t.matmul(_p, _w, _x, start=st, stop=sp),
                                incs="pe" if dtl == 15 else None) or cp
                    if tg == 3:
                        apb_use[slot] = cp
                    if ft < 2:
                        dst = uni[:, U_QT + ft * 2048 + tg * 512:U_QT + ft * 2048 + (tg + 1) * 512]
                    elif ft < 4:
                        dst = uni[:, U_KT + (ft - 2) * 2048 + tg * 512:U_KT + (ft - 2) * 2048 + (tg + 1) * 512]
                    else:
                        dst = uni[:, U_VT + (ft - 4) * 2048 + tg * 512:U_VT + (ft - 4) * 2048 + (tg + 1) * 512]
                    ev = OP("scalar", lambda s, o=dst, i2=pp[:], bi=bqkv_sb[:, ft:ft + 1]:
                            s.activation(out=o, in_=i2, func=AF.Identity, bias=bi),
                            incs="act", waits=[cp])
                    p2f[g] = ev
                    last_ev = ev
            qk_ev[0] = last_ev

        def emit_B3(c):
            Wl("tensor", [qk_ev[0]])
            for h in range(2):
                for half in range(2):
                    pd = P3[half]
                    Wl("tensor", [p3f[half]])
                    ptr = None
                    for k in range(8):
                        tk = half * 8 + k
                        ptr = OP("tensor", lambda t, p=pd[:, k * 128:(k + 1) * 128],
                                 i2=uni[:, U_VT + h * 2048 + tk * 128:U_VT + h * 2048 + (tk + 1) * 128]:
                                 t.transpose(p, i2, id_sb[:]),
                                 incs="pe" if k == 7 else None) or ptr
                    dstv = uni[:, U_V + half * 2048:U_V + (half + 1) * 2048].rearrange(
                        "p (a r) -> p a r", r=256)[:, :, h * 128:(h + 1) * 128]
                    ev = OP("scalar", lambda s, o=dstv,
                            i2=pd[:].rearrange("p (a r) -> p a r", r=128):
                            s.copy(out=o, in_=i2), incs="act", waits=[ptr])
                    p3f[half] = ev

        def emit_B4(c, h, avd):
            prev_nm = Z
            ptev = Z
            prev_ea = prev_eb = Z
            for q in range(16):
                if q == 0:
                    Wl("tensor", [qk_ev[0]])
                    for k in range(4):
                        Wl("tensor", p1f[k])
                sc = [None, None]
                for kt in range(4):
                    if q > 0 and kt in (0, 2):
                        Wl("tensor", [prev_ea if kt == 0 else prev_eb] +
                           ([prev_nm] if kt == 0 else []))
                    r_ = OP("tensor", lambda t, _p=P1[:, kt * 512:(kt + 1) * 512],
                            _q=uni[:, U_QT + h * 2048 + q * 128:U_QT + h * 2048 + (q + 1) * 128],
                            _k=uni[:, U_KT + h * 2048 + kt * 512:U_KT + h * 2048 + (kt + 1) * 512]:
                            t.matmul(_p, _q, _k, start=True, stop=True),
                            incs="pe" if kt in (1, 3) else None)
                    if kt == 1:
                        sc[0] = r_
                    elif kt == 3:
                        sc[1] = r_
                rmx = OP("vector", lambda v: v.reduce_max(smal[:, 6:7], P1[:, 0:512], axis=AX.X,
                         negate=True), incs="dve", waits=[sc[0], prev_ea])
                nm = OP("vector", lambda v: v.tensor_single_scalar(smal[:, 7:8], smal[:, 6:7],
                        ISQ, A.mult), incs="dve", waits=[rmx])
                ea = OP("scalar", lambda s: s.activation(out=uni[:, U_P:U_P + 1024],
                        in_=P1[:, 0:1024], func=AF.Exp, scale=ISQ, bias=smal[:, 7:8],
                        accum_out=smal[:, 8:9]), incs="act", waits=[sc[0], nm])
                eb = OP("scalar", lambda s: s.activation(out=uni[:, U_P + 1024:U_P + 2048],
                        in_=P1[:, 1024:2048], func=AF.Exp, scale=ISQ, bias=smal[:, 7:8],
                        accum_out=smal[:, 14:15]), incs="act", waits=[sc[1]])
                sad = OP("vector", lambda v: v.tensor_tensor(smal[:, 9:10], smal[:, 8:9],
                         smal[:, 14:15], A.add), incs="dve", waits=[eb])
                rcp = OP("vector", lambda v: v.reciprocal(smal[:, 9:10], smal[:, 9:10]),
                         incs="dve", waits=[sad])
                pn = OP("vector", lambda v: v.tensor_single_scalar(uni[:, U_PN:U_PN + 2048],
                        uni[:, U_P:U_P + 2048], smal[:, 9:10], A.mult), incs="dve",
                        waits=[rcp])
                prev_ea, prev_eb, prev_nm = ea, eb, nm
                ql = q % 4
                Wl("tensor", [pn])
                for half in range(2):
                    pd = P3[half]
                    Wl("tensor", [p3f[half]])
                    ptr = None
                    for k in range(8):
                        kt = half * 8 + k
                        ptr = OP("tensor", lambda t, p=pd[:, k * 128:(k + 1) * 128],
                                 i2=uni[:, U_PN + kt * 128:U_PN + (kt + 1) * 128]:
                                 t.transpose(p, i2, id_sb[:]),
                                 incs="pe" if k == 7 else None) or ptr
                    dstv = uni[:, U_PT + half * 8 * 512:U_PT + (half * 8 + 8) * 512].rearrange(
                        "p (a r) -> p a r", r=512)[:, :, ql * 128:(ql + 1) * 128]
                    if half == 0:
                        ev = OP("vector", lambda v, o=dstv,
                                i2=pd[:].rearrange("p (a r) -> p a r", r=128):
                                v.tensor_copy(o, i2), incs="dve",
                                waits=[ptr, avd])
                    else:
                        ev = OP("scalar", lambda s, o=dstv,
                                i2=pd[:].rearrange("p (a r) -> p a r", r=128):
                                s.copy(out=o, in_=i2), incs="act", waits=[ptr])
                    p3f[half] = ev
                    ptev = ev
                if ql == 3:
                    g4 = q // 4
                    gs = g4 % 2
                    Wl("tensor", [ptev, p2f[gs]])
                    for kt in range(16):
                        r_ = OP("tensor", lambda t, _p=P2[gs][:],
                                _v=uni[:, U_V + kt * 256 + h * 128:U_V + kt * 256 + (h + 1) * 128],
                                _pt=uni[:, U_PT + kt * 512:U_PT + (kt + 1) * 512],
                                st=(kt == 0), sp=(kt == 15): t.matmul(_p, _v, _pt, start=st, stop=sp),
                                incs="pe" if kt == 15 else None)
                        avd = r_ or avd
                    ev = OP("scalar", lambda s, o=ao[:, h * 2048 + g4 * 512:h * 2048 + (g4 + 1) * 512],
                            i2=P2[gs][:]: s.copy(out=o, in_=i2), incs="act", waits=[avd])
                    p2f[gs] = ev
                    ao_ev[0] = ev
            p1f[0] = [prev_ea, prev_nm]
            p1f[1] = [prev_ea]
            p1f[2] = [prev_eb]
            p1f[3] = [prev_eb]
            return avd

        def emit_B5(c, avd, dec_per_j=0):
            first = True
            for j in range(16):
                if dec_per_j:
                    emit_decode(dec_per_j)
                ldg = DMA("sync", wpool[:, (j % 2) * 2048:(j % 2 + 1) * 2048], dec["g"][j],
                          waits=[dec_store[("g", j)], wslot_use[j % 2]])
                ldu = DMA("sync", wpool[:, (2 + j % 2) * 2048:(3 + j % 2) * 2048], dec["u"][j],
                          waits=[dec_store[("u", j)], wslot_use[2 + j % 2]])
                for tg in range(4):
                    qb = tg % 2
                    bg = P1[:, (2 * qb) * 512:(2 * qb + 1) * 512]
                    bu = P1[:, (2 * qb + 1) * 512:(2 * qb + 2) * 512]
                    Wl("tensor", [ldg, ldu] + xnt_all[c])
                    Wl("tensor", p1f[2 * qb] + p1f[2 * qb + 1])
                    gmm = umm = None
                    for kd in range(16):
                        gmm = OP("tensor", lambda t, _p=bg, _w=wpool[:, (j % 2) * 2048 + kd * 128:
                                 (j % 2) * 2048 + (kd + 1) * 128],
                                 _x=xnt[:, kd * 2048 + tg * 512:kd * 2048 + (tg + 1) * 512],
                                 st=(kd == 0), sp=(kd == 15): t.matmul(_p, _w, _x, start=st, stop=sp),
                                 incs="pe" if kd == 15 else None) or gmm
                    for kd in range(16):
                        umm = OP("tensor", lambda t, _p=bu, _w=wpool[:, (2 + j % 2) * 2048 + kd * 128:
                                 (2 + j % 2) * 2048 + (kd + 1) * 128],
                                 _x=xnt[:, kd * 2048 + tg * 512:kd * 2048 + (tg + 1) * 512],
                                 st=(kd == 0), sp=(kd == 15): t.matmul(_p, _w, _x, start=st, stop=sp),
                                 incs="pe" if kd == 15 else None) or umm
                    if tg == 3:
                        wslot_use[j % 2] = gmm
                        wslot_use[2 + j % 2] = umm
                    geld = OP("scalar", lambda s, _g=bg, o=gg[:, qb * 512:(qb + 1) * 512]:
                              s.activation(out=o, in_=_g, func=AF.Gelu_apprx_tanh),
                              incs="act", waits=[gmm, gg_free[qb]])
                    p1f[2 * qb] = [geld]
                    iw = [umm, geld]
                    if first:
                        iw += [avd, ao_ev[0]]
                        first = False
                    intd = OP("vector", lambda v, _u=bu,
                              o=uni[:, j * 2048 + tg * 512:j * 2048 + (tg + 1) * 512],
                              g2=gg[:, qb * 512:(qb + 1) * 512]:
                              v.tensor_tensor(o, g2, _u, A.mult), incs="dve", waits=iw)
                    p1f[2 * qb + 1] = [intd]
                    gg_free[qb] = intd
                while dec_pend:
                    emit_decode_pe()
            b5_last[c] = cnt["pe"]
            b5_intd[0] = cnt["dve"]

        def emit_B6(c):
            W("tensor", "act", ao_ev[0][1])
            W("tensor", "dve", b5_intd[0])
            if c == 0:
                W("tensor", "dve", a1_all)
            wo_toks = []
            for kt in range(2):
                wo_toks.append(DMA("gpsimd", wpool[:, (2 + kt) * 2048:(3 + kt) * 2048],
                                   dec_a[6 + kt], waits=deca_st[6 + kt] + [wslot_use[2 + kt]]))
            part_last = None
            rs_half = [None, None]
            for idx, dtl in enumerate(list(range(0, 16, 2)) + list(range(1, 16, 2))):
                slot = idx % 2
                ldd = DMA("gpsimd", wpool[:, slot * 2048:(slot + 1) * 2048],
                          dec["d"][dtl],
                          waits=[dec_store[("d", dtl)], wslot_use[slot]])
                for tg in range(4):
                    gs = (4 * dtl + tg) % 2
                    pp = P2[gs]
                    Wl("tensor", [ldd] + wo_toks)
                    Wl("tensor", [p2f[gs]])
                    for kt in range(2):
                        OP("tensor", lambda t, _p=pp[:],
                           _w=wpool[:, (2 + kt) * 2048 + dtl * 128:(2 + kt) * 2048 + (dtl + 1) * 128],
                           _a=ao[:, kt * 2048 + tg * 512:kt * 2048 + (tg + 1) * 512],
                           st=(kt == 0): t.matmul(_p, _w, _a, start=st, stop=False))
                    pdn = None
                    for kn in range(16):
                        pdn = OP("tensor", lambda t, _p=pp[:],
                                 _w=wpool[:, slot * 2048 + kn * 128:slot * 2048 + (kn + 1) * 128],
                                 _i=uni[:, kn * 2048 + tg * 512:kn * 2048 + (tg + 1) * 512],
                                 sp=(kn == 15): t.matmul(_p, _w, _i, start=False, stop=sp),
                                 incs="pe" if kn == 15 else None) or pdn
                    if tg == 3:
                        wslot_use[slot] = pdn
                        if dtl == 15:
                            wslot_use[2] = pdn
                            wslot_use[3] = pdn
                    ypc = OP("scalar", lambda s, o=yp[:, gs * 512:(gs + 1) * 512], i2=pp[:]:
                             s.copy(out=o, in_=i2), incs="act",
                             waits=[pdn, yp_free[gs]])
                    p2f[gs] = ypc
                    part_last = DMA("scalar",
                                    part[c][dtl % 2][(dtl // 2) * 128:(dtl // 2 + 1) * 128,
                                                     tg * 512:(tg + 1) * 512],
                                    yp[:, gs * 512:(gs + 1) * 512], waits=[ypc])
                    yp_free[gs] = part_last
                if idx in (7, 15):
                    h = idx // 8
                    Wl("gpsimd", qmark("scalar"))
                    rs_half[h] = OP("gpsimd", lambda g, _c=c, _h=h: g.collective_compute(
                        "ReduceScatter", A.add, replica_groups=[list(range(R))],
                        ins=[part[_c][_h].opt()], outs=[rs_o[_c][_h].opt()]), incs="cc")
            rs_done[c] = rs_half

        def emit_F(c):
            seen = set()
            stv = None
            for pc in range(8):
                dtl, tg = pc // 4, pc % 4
                slot = pc % 2
                rsb = fnb[:, slot * 1536:slot * 1536 + 512]
                xtb = fnb[:, slot * 1536 + 512:slot * 1536 + 1024]
                yb = fnb[:, slot * 1536 + 1024:slot * 1536 + 1536]
                firstw = [("cc", rs_done[c][dtl][1])] if dtl not in seen else []
                seen.add(dtl)
                l1 = DMA("gpsimd", rsb,
                         rs_o[c][dtl][0:128, tg * 512:(tg + 1) * 512],
                         waits=firstw + [fn_free[slot]])
                l2 = DMA("gpsimd", xtb,
                         xT_s[dtl * 128:(dtl + 1) * 128, c * 2048 + tg * 512:c * 2048 + (tg + 1) * 512])
                yd = OP("vector", lambda v, _r=rsb, _x=xtb, _y=yb, bi=bo_sb[:, dtl:dtl + 1]:
                        v.scalar_tensor_tensor(_y, _r, bi, _x, A.add, A.add),
                        incs="dve", waits=[l1, l2])
                stv = DMA("gpsimd",
                          yT_s[dtl * 128:(dtl + 1) * 128, c * 2048 + tg * 512:c * 2048 + (tg + 1) * 512],
                          yb, waits=[yd])
                fn_free[slot] = yd
            return stv

        for pi in range(4):
            emit_decode(6)
            emit_A1(pi)
        emit_decode(8)
        a1_all = cnt["dve"]
        a1_st = qmark("scalar")
        emit_xnt_load(0)
        emit_B2(0)
        emit_B3(0)
        avd = emit_B4(0, 0, Z)
        avd = emit_B4(0, 1, avd)
        emit_B5(0, avd, dec_per_j=1)
        emit_B6(0)
        for c in range(1, 4):
            emit_xnt_load(c)
            emit_B2(c)
            emit_B3(c)
            avd = emit_B4(c, 0, avd)
            avd = emit_B4(c, 1, avd)
            emit_B5(c, avd)
            emit_B6(c)
            emit_F(c - 1)
        last_st = emit_F(3)
        Wl("gpsimd", [last_st])
        Wl("gpsimd", qmark("gpsimd"))

        with nc.Block() as block:
            @block.sync
            def _(e):
                for f in ops["sync"]:
                    f(e)

            @block.vector
            def _(e):
                for f in ops["vector"]:
                    f(e)

            @block.scalar
            def _(e):
                for f in ops["scalar"]:
                    f(e)

            @block.tensor
            def _(e):
                for f in ops["tensor"]:
                    f(e)

            @block.gpsimd
            def _(e):
                for f in ops["gpsimd"]:
                    f(e)
    return nc


_CACHE = {}


def kernel(L0_attn_x, L0_attn_wq_codes, L0_attn_wq_scales, L0_attn_bq,
           L0_attn_wk_codes, L0_attn_wk_scales, L0_attn_bk,
           L0_attn_wv_codes, L0_attn_wv_scales, L0_attn_bv,
           L0_attn_wo_codes, L0_attn_wo_scales, L0_attn_bo,
           L0_ffn_gate_W_packed, L0_ffn_gate_scales,
           L0_ffn_up_W_packed, L0_ffn_up_scales,
           L0_ffn_down_W_packed, L0_ffn_down_scales):
    if "nc" not in _CACHE:
        _CACHE["nc"] = build_nc()
    nc = _CACHE["nc"]

    ac = np.ascontiguousarray
    xf = np.asarray(L0_attn_x, dtype=np.float32).reshape(T, D)
    xT = ac(xf.T)
    rep = np.repeat(np.arange(64), 32)
    wq_se_f = np.asarray(L0_attn_wq_scales)[rep, :]
    wk_se_f = np.asarray(L0_attn_wk_scales)[rep, :]
    wv_se_f = np.asarray(L0_attn_wv_scales)[rep, :]
    wo_se_f = np.asarray(L0_attn_wo_scales)[rep, :]
    ident = np.eye(128, dtype=ml_dtypes.bfloat16)
    bq = np.asarray(L0_attn_bq)
    bk = np.asarray(L0_attn_bk)
    bv = np.asarray(L0_attn_bv)
    bo = np.asarray(L0_attn_bo)

    in_maps = []
    for c in range(R):
        f0 = FT * c
        n0 = 2048 * c
        m = {
            "x_tok": ac(xf[TKC * c:TKC * (c + 1)]),
            "xT_s": ac(xT[f0:f0 + FT]),
            "wq_c": ac(np.asarray(L0_attn_wq_codes)[:, f0:f0 + FT]),
            "wk_c": ac(np.asarray(L0_attn_wk_codes)[:, f0:f0 + FT]),
            "wv_c": ac(np.asarray(L0_attn_wv_codes)[:, f0:f0 + FT]),
            "wq_se": ac(wq_se_f[:, f0:f0 + FT]),
            "wk_se": ac(wk_se_f[:, f0:f0 + FT]),
            "wv_se": ac(wv_se_f[:, f0:f0 + FT]),
            "wo_c": ac(np.asarray(L0_attn_wo_codes)[f0:f0 + FT, :]),
            "wo_se": ac(wo_se_f[f0:f0 + FT, :]),
            "bqkv": ac(np.stack([bq[f0:f0 + 128], bq[f0 + 128:f0 + 256],
                                 bk[f0:f0 + 128], bk[f0 + 128:f0 + 256],
                                 bv[f0:f0 + 128], bv[f0 + 128:f0 + 256]], axis=1)),
            "bo_s": ac(np.stack([bo[f0:f0 + 128], bo[f0 + 128:f0 + 256]], axis=1)),
            "g_p": ac(np.asarray(L0_ffn_gate_W_packed)[n0:n0 + 2048]),
            "g_s": ac(np.asarray(L0_ffn_gate_scales)[n0:n0 + 2048]),
            "u_p": ac(np.asarray(L0_ffn_up_W_packed)[n0:n0 + 2048]),
            "u_s": ac(np.asarray(L0_ffn_up_scales)[n0:n0 + 2048]),
            "d_p": ac(np.asarray(L0_ffn_down_W_packed)[:, 1024 * c:1024 * (c + 1)]),
            "d_s": ac(np.asarray(L0_ffn_down_scales)[:, 64 * c:64 * (c + 1)]),
            "ident": ident,
        }
        in_maps.append(m)

    res = run_bass_kernel_spmd(nc, in_maps, core_ids=list(range(R)))
    shards = [np.asarray(res.results[c]["yT_s"]) for c in range(R)]
    yT = np.concatenate(shards, axis=0)
    return ac(yT.T).reshape(B_, S, D)


# version 24
# speedup vs baseline: 1.0704x; 1.0030x over previous
import numpy as np
import ml_dtypes
from contextlib import ExitStack
import concourse.bass as bass
import concourse.mybir as mybir
from concourse.bass_utils import run_bass_kernel_spmd

F32 = mybir.dt.float32
BF16 = mybir.dt.bfloat16
I32 = mybir.dt.int32
U16 = mybir.dt.uint16
A = mybir.AluOpType
AF = mybir.ActivationFunctionType
AX = mybir.AxisListType

R = 8
B_, S, D = 4, 2048, 2048
T = B_ * S
TKC = T // R
FT = 256
ISQ = 0.08838834764831845
NQS = 4


def build_nc():
    nc = bass.Bass("TRN2", target_bir_lowering=False)

    def din(name, shape, dt):
        return nc.dram_tensor(name, shape, dt, kind="ExternalInput").ap()

    def dint(name, shape, dt):
        return nc.dram_tensor(name, shape, dt, kind="Internal").ap()

    x_tok = din("x_tok", [TKC, D], F32)
    xT_s = din("xT_s", [FT, T], F32)
    w_qkv_c = [din(f"w{p}_c", [D, FT], F32) for p in "qkv"]
    w_qkv_se = [din(f"w{p}_se", [D, FT], F32) for p in "qkv"]
    wo_c = din("wo_c", [FT, D], F32)
    wo_se = din("wo_se", [FT, D], F32)
    bqkv_h = din("bqkv", [128, 6], F32)
    bo_h = din("bo_s", [128, 2], F32)
    ffn_p = {w: din(f"{w}_p", [2048, 1024], I32) for w in ("g", "u", "d")}
    ffn_s = {w: din(f"{w}_s", [2048, 64], F32) for w in ("g", "u", "d")}
    ident_h = din("ident", [128, 128], BF16)
    yT_s = nc.dram_tensor("yT_s", [FT, T], F32, kind="ExternalOutput").ap()

    ag_in = [dint(f"ag_in{h}", [D, 512], BF16) for h in range(2)]
    ag_out = [dint(f"ag_out{h}", [R * D, 512], BF16) for h in range(2)]
    dec = {w: dint(f"dec_{w}", [16, 128, 2048], BF16) for w in ("g", "u", "d")}
    dec_a = dint("dec_a", [8, 128, 2048], BF16)
    part = [[dint(f"part{c}h{h}", [D // 2, S], F32) for h in range(2)] for c in range(4)]
    rs_o = [[dint(f"rs_o{c}h{h}", [128, S], F32) for h in range(2)] for c in range(4)]

    with ExitStack() as ctx:
        def sbuf(name, shape, dt):
            return ctx.enter_context(nc.sbuf_tensor(name, shape, dt))

        def psum(name, shape, dt):
            return ctx.enter_context(nc.psum_tensor(name, shape, dt))

        xnt = sbuf("xnt", [128, 32768], BF16)
        uni = sbuf("uni", [128, 32768], BF16)
        ao = sbuf("ao", [128, 4096], BF16)
        wpool = sbuf("wpool", [128, 8192], BF16)
        apb = sbuf("apb", [128, 4096], BF16)
        pk = sbuf("pk", [128, 2048], I32)
        dtmp = sbuf("dtmp", [128, 4096], I32)
        codes = sbuf("codes", [128, 2048], U16)
        dq = sbuf("dq", [128, 2048], BF16)
        stage = sbuf("stage", [128, 4096], BF16)
        scb = sbuf("scb", [128, 128], F32)
        id_sb = sbuf("id_sb", [128, 128], BF16)
        bqkv_sb = sbuf("bqkv_sb", [128, 6], F32)
        bo_sb = sbuf("bo_sb", [128, 2], F32)
        gg = sbuf("gg", [128, 1024], BF16)
        yp = sbuf("yp", [128, 1024], F32)
        smal = sbuf("smal", [128, 16], F32)
        xn_t = None

        P1 = psum("P1", [128, 2048], F32)
        P2 = [psum("P2a", [128, 512], F32), psum("P2b", [128, 512], F32)]
        P3 = [psum("P3a", [128, 1024], BF16), psum("P3b", [128, 1024], BF16)]

        qsems = {q: [f"{q}{i}" for i in range(NQS)] for q in ("sync", "scalar", "gpsimd")}
        semnames = ["cc", "pe", "dve", "act"] + sum(qsems.values(), [])
        sem = {n: ctx.enter_context(nc.semaphore(n)) for n in semnames}
        cnt = {n: 0 for n in semnames}
        qrot = {q: 0 for q in qsems}
        ops = {"sync": [], "vector": [], "scalar": [], "tensor": [], "gpsimd": []}

        def em(eng, fn):
            ops[eng].append(fn)

        def W(eng, s, v):
            if v > 0:
                em(eng, lambda e, _s=sem[s], _v=v: e.wait_ge(_s, _v))

        def Wl(eng, toks):
            for (s, v) in toks:
                W(eng, s, v)

        def inc(s, n=1):
            cnt[s] += n
            return cnt[s]

        def DMA(q, out_ap, in_ap, waits=()):
            Wl(q, waits)
            sname = qsems[q][qrot[q] % NQS]
            qrot[q] += 1
            c = inc(sname, 16)
            em(q, lambda e, o=out_ap, i=in_ap, sm=sem[sname]:
               e.dma_start(out=o, in_=i).then_inc(sm, 16))
            return (sname, c)

        def qmark(q):
            return [(s, cnt[s]) for s in qsems[q] if cnt[s] > 0]

        def OP(eng, build, incs=None, waits=()):
            Wl(eng, waits)
            if incs:
                c = inc(incs)
                em(eng, lambda e, b=build, sm=sem[incs]: b(e).then_inc(sm, 1))
                return (incs, c)
            em(eng, lambda e, b=build: b(e))
            return None

        u_f32 = uni[:].bitcast(F32)
        wp_f32 = wpool[:].bitcast(F32)
        jnk = codes[:].bitcast(BF16)
        fnb = dtmp[:].bitcast(F32)

        xn_t = stage[:, 0:2048]
        U_QT, U_KT, U_V, U_P, U_PN, U_VT, U_PT = 0, 4096, 8192, 12288, 14336, 16384, 16384
        Z = ("pe", 0)

        p1f = [[Z], [Z], [Z], [Z]]
        p2f = [Z, Z]
        p3f = [Z, Z]

        for o, i in ((id_sb, ident_h), (bqkv_sb, bqkv_h), (bo_sb, bo_h)):
            DMA("sync", o[:], i)
        con_ld = qmark("sync")

        OP("vector", lambda v: v.memset(smal[:, 10:11], 1e-5))
        OP("vector", lambda v: v.memset(smal[:, 11:12], 16128.0), incs="dve")
        ln_done = []
        ag_done = [None, None]
        pe_tr = Z
        for tt in range(8):
            slot = tt % 2
            xsl = wp_f32[:, slot * 2048:(slot + 1) * 2048]
            ldw = [ln_done[tt - 2]] if tt >= 2 else ()
            ldv = DMA("sync", xsl, x_tok[tt * 128:(tt + 1) * 128, :], waits=ldw)
            rs1 = OP("vector", lambda v, x=xsl: v.reduce_sum(smal[:, 0:1], x, axis=AX.X),
                     incs="dve", waits=[ldv])
            nmn = OP("vector", lambda v: v.tensor_single_scalar(smal[:, 2:3], smal[:, 0:1],
                     -1.0 / D, A.mult), incs="dve", waits=[rs1])
            sq = OP("scalar", lambda s, x=xsl: s.activation(out=jnk[:], in_=x, func=AF.Square,
                    accum_out=smal[:, 1:2]), incs="act", waits=[ldv])
            OP("vector", lambda v: v.tensor_single_scalar(smal[:, 3:4], smal[:, 1:2],
                                                          1.0 / D, A.mult), waits=[sq, nmn])
            m2c = OP("vector", lambda v: v.tensor_tensor(smal[:, 4:5], smal[:, 2:3],
                     smal[:, 2:3], A.mult), incs="dve")
            vd = OP("vector", lambda v: v.tensor_tensor(smal[:, 12:13], smal[:, 3:4],
                    smal[:, 4:5], A.subtract), incs="dve", waits=[m2c])
            sdc = OP("scalar", lambda s: s.activation(out=smal[:, 13:14], in_=smal[:, 12:13],
                     func=AF.Sqrt, bias=smal[:, 10:11], scale=1.0), incs="act", waits=[vd])
            rcc = OP("vector", lambda v: v.reciprocal(smal[:, 5:6], smal[:, 13:14]),
                     incs="dve", waits=[sdc])
            xw = [rcc] + ([pe_tr] if tt >= 1 else [])
            xnd = OP("vector", lambda v, x=xsl: v.tensor_scalar(xn_t[:], x, smal[:, 2:3],
                     smal[:, 5:6], A.add, A.mult), incs="dve", waits=xw)
            ln_done.append(xnd)
            Wl("tensor", [xnd])
            if tt == 0:
                Wl("tensor", con_ld)
            for half in range(2):
                pd = P3[half]
                Wl("tensor", [p3f[half]])
                for k in range(8):
                    dtl = half * 8 + k
                    pe_tr = OP("tensor", lambda t, p=pd[:, k * 128:(k + 1) * 128],
                               i=xn_t[:, dtl * 128:(dtl + 1) * 128]:
                               t.transpose(p, i, id_sb[:]),
                               incs="pe" if k == 7 else None) or pe_tr
                dst = uni[:].rearrange("p (dt q) -> p dt q", q=1024)[
                    :, half * 8:half * 8 + 8, tt * 128:(tt + 1) * 128]
                ev = OP("scalar", lambda s, o=dst,
                        i=pd[:].rearrange("p (dt q) -> p dt q", q=128):
                        s.copy(out=o, in_=i), incs="act", waits=[pe_tr])
                p3f[half] = ev
            if tt in (3, 7):
                th = tt // 4
                W("gpsimd", "act", cnt["act"])
                for dtl in range(16):
                    DMA("gpsimd", ag_in[th][dtl * 128:(dtl + 1) * 128, :],
                        uni[:, dtl * 1024 + th * 512:dtl * 1024 + th * 512 + 512])
                Wl("gpsimd", qmark("gpsimd"))
                ag_done[th] = OP("gpsimd", lambda g, _h=th: g.collective_compute(
                    "AllGather", A.bypass, replica_groups=[list(range(R))],
                    ins=[ag_in[_h].opt()], outs=[ag_out[_h].opt()]), incs="cc")
        ag_st = qmark("gpsimd")

        st_free2 = [("act", 0), ("act", 0), ("act", 0), ("act", 0)]
        a1_srcs = [(w_qkv_c[0], w_qkv_se[0]), (w_qkv_c[1], w_qkv_se[1]),
                   (w_qkv_c[2], w_qkv_se[2]), (wo_c, wo_se)]
        a1_ttp = [None] * 4
        deca_st = {}
        a1_state = {"sidx": 0}

        def emit_A1(pi):
            sidx = a1_state["sidx"]
            ch, sh = a1_srcs[pi]
            base = (pi % 2) * 8192
            lw = list(ag_st) if pi < 2 else [a1_ttp[pi - 2]]
            toks = []
            if pi < 3:
                cin = ch[:].rearrange("(dt p) f -> p dt f", p=128)
                sin = sh[:].rearrange("(dt p) f -> p dt f", p=128)
                toks.append(DMA("scalar",
                                u_f32[:, base:base + 4096].rearrange("p (dt f) -> p dt f", f=FT),
                                cin, waits=lw))
                toks.append(DMA("scalar",
                                u_f32[:, base + 4096:base + 8192].rearrange("p (dt f) -> p dt f", f=FT),
                                sin))
            else:
                toks.append(DMA("scalar", u_f32[:, base:base + 2048], ch[0:128, :], waits=lw))
                toks.append(DMA("scalar", u_f32[:, base + 2048:base + 4096], ch[128:256, :]))
                toks.append(DMA("scalar", u_f32[:, base + 4096:base + 6144], sh[0:128, :]))
                toks.append(DMA("scalar", u_f32[:, base + 6144:base + 8192], sh[128:256, :]))
            Wl("vector", toks)
            a1_tt = None
            for f in range(2):
                for hg in range(2):
                    ss = sidx % 4
                    sidx += 1
                    Wl("vector", [st_free2[ss]])
                    for k in range(8):
                        dtl = hg * 8 + k
                        if pi < 3:
                            a_ = u_f32[:, base + dtl * 256 + f * 128:base + dtl * 256 + f * 128 + 128]
                            b_ = u_f32[:, base + 4096 + dtl * 256 + f * 128:base + 4096 + dtl * 256 + f * 128 + 128]
                        else:
                            a_ = u_f32[:, base + f * 2048 + dtl * 128:base + f * 2048 + (dtl + 1) * 128]
                            b_ = u_f32[:, base + 4096 + f * 2048 + dtl * 128:base + 4096 + f * 2048 + (dtl + 1) * 128]
                        o = stage[:, ss * 1024 + k * 128:ss * 1024 + (k + 1) * 128]
                        a1_tt = OP("vector", lambda v, _o=o, _a=a_, _b=b_:
                                   v.tensor_tensor(_o, _a, _b, A.mult),
                                   incs="dve" if k == 7 else None) or a1_tt
                    stv = DMA("scalar", dec_a[2 * pi + f][:, hg * 1024:(hg + 1) * 1024],
                              stage[:, ss * 1024:(ss + 1) * 1024], waits=[a1_tt])
                    st_free2[ss] = stv
                    deca_st.setdefault(2 * pi + f, []).append(stv)
            a1_ttp[pi] = a1_tt
            a1_state["sidx"] = sidx

        Tt = [dtmp[:, i * 1024:(i + 1) * 1024] for i in range(4)]
        pk_free = [("dve", 0), ("dve", 0)]
        sc_free = [("dve", 0), ("dve", 0)]
        st_free = st_free2
        dec_store = {}
        dec_ptr = {}
        dec_i = [0]
        dec_pend = []
        dec_order = [(w, rt) for rt in range(16) for w in ("g", "u")] + \
                    [("d", rt) for rt in range(16)]
        p0_act = cnt["act"]

        def emit_decode_dve(n):
            for _ in range(n):
                if dec_i[0] >= len(dec_order):
                    return
                i = dec_i[0]
                dec_i[0] += 1
                w, rt = dec_order[i]
                slot = i % 2
                pks = pk[:, slot * 1024:(slot + 1) * 1024]
                scs = scb[:, slot * 64:(slot + 1) * 64]
                pkv = DMA("gpsimd", pks, ffn_p[w][rt * 128:(rt + 1) * 128, :],
                          waits=[pk_free[slot]])
                scv = DMA("gpsimd", scs, ffn_s[w][rt * 128:(rt + 1) * 128, :],
                          waits=[sc_free[slot]])
                w0 = [pkv] + ([("act", p0_act)] if i == 0 else [])
                dcd = None
                for lohi in range(2):
                    if lohi == 0:
                        OP("vector", lambda v, p=pks: v.tensor_single_scalar(
                           Tt[0][:], p, 7, A.bitwise_and), waits=w0)
                        OP("vector", lambda v, p=pks: v.tensor_single_scalar(
                           Tt[1][:], p, 8, A.bitwise_and))
                    else:
                        OP("vector", lambda v, p=pks: v.tensor_scalar(
                           Tt[0][:], p, 4, 7, A.logical_shift_right, A.bitwise_and))
                        pkfree = OP("vector", lambda v, p=pks: v.tensor_scalar(
                                    Tt[1][:], p, 4, 8, A.logical_shift_right, A.bitwise_and),
                                    incs="dve")
                        pk_free[slot] = pkfree
                    bbv = OP("vector", lambda v: v.tensor_scalar(Tt[2][:], Tt[0][:],
                             64, 16128, A.mult, A.add), incs="dve")
                    mmn = OP("vector", lambda v: v.scalar_tensor_tensor(Tt[3][:], Tt[0][:],
                             16128.0, Tt[2][:], A.mult, A.min), incs="dve", waits=[bbv])
                    dst = codes[:, lohi:2048:2]
                    dcd = OP("vector", lambda v, _d=dst: v.scalar_tensor_tensor(
                             _d, Tt[1][:], 4096.0, Tt[3][:], A.mult, A.add),
                             incs="dve", waits=[mmn])
                dqd = OP("vector", lambda v, _s=scs:
                         v.tensor_tensor(dq[:].rearrange("p (j r) -> p j r", r=32),
                                         codes[:].bitcast(BF16).rearrange("p (j r) -> p j r", r=32),
                                         _s.unsqueeze(2).broadcast_to([128, 64, 32]),
                                         A.mult), incs="dve",
                         waits=[scv, dcd, dec_ptr.get(i - 1, Z)])
                sc_free[slot] = dqd
                dec_pend.append((i, w, rt, dqd))

        def emit_decode_pe():
            i, w, rt, dqd = dec_pend.pop(0)
            Wl("tensor", [dqd])
            ptr = None
            for half in range(2):
                pd = P3[half]
                Wl("tensor", [p3f[half]])
                for k in range(8):
                    kt2 = half * 8 + k
                    ptr = OP("tensor", lambda t, p=pd[:, k * 128:(k + 1) * 128],
                             i2=dq[:, kt2 * 128:(kt2 + 1) * 128]:
                             t.transpose(p, i2, id_sb[:]),
                             incs="pe" if k == 7 else None) or ptr
                ss2 = (2 * i + half) % 4
                ev = OP("scalar", lambda s, o=stage[:, ss2 * 1024:(ss2 + 1) * 1024],
                        i2=pd[:]: s.copy(out=o, in_=i2), incs="act",
                        waits=[ptr, st_free[ss2]])
                p3f[half] = ev
                stv = DMA("scalar", dec[w][rt][:, half * 1024:(half + 1) * 1024],
                          stage[:, ss2 * 1024:(ss2 + 1) * 1024], waits=[ev])
                st_free[ss2] = stv
            dec_ptr[i] = ptr
            dec_store[(w, rt)] = stv

        def emit_decode(n):
            for _ in range(n):
                if dec_i[0] >= len(dec_order):
                    return
                emit_decode_dve(1)
                emit_decode_pe()

        xnt_h0 = [None] * 4
        xnt_all = [None] * 4
        apb_use = [Z, Z]
        wslot_use = [Z, Z, Z, Z]
        yp_free = [("act", 0), ("act", 0)]
        b5_last = [0, 0, 0, 0]
        rs_done = [0, 0, 0, 0]
        fn_free = [("dve", 0), ("dve", 0)]
        gg_free = [("dve", 0), ("dve", 0)]
        qk_ev = [Z]
        ao_ev = [Z]
        b5_intd = [0]

        def emit_xnt_load(c):
            w0 = [("pe", b5_last[c - 1])] if c > 0 else []
            for th in range(2):
                waits = w0 + ([("cc", ag_done[th][1])] if c == 0 else [])
                w0 = []
                for dtl in range(16):
                    for h2 in range(2):
                        DMA("sync",
                            xnt[:, dtl * 2048 + h2 * 1024 + th * 512:
                                dtl * 2048 + h2 * 1024 + th * 512 + 512],
                            ag_out[th][(2 * c + h2) * D + dtl * 128:
                                       (2 * c + h2) * D + (dtl + 1) * 128, :],
                            waits=waits)
                        waits = ()
                if th == 0:
                    xnt_h0[c] = qmark("sync")
            xnt_all[c] = qmark("sync")

        def emit_B2(c):
            last_ev = Z
            for ft in range(6):
                slot = ft % 2
                apb_ld = DMA("sync", apb[:, slot * 2048:(slot + 1) * 2048],
                             dec_a[ft], waits=deca_st[ft] + [apb_use[slot]])
                for tg in (0, 2, 1, 3):
                    g = (4 * ft + tg) % 2
                    pp = P2[g]
                    Wl("tensor", [apb_ld] + (xnt_h0[c] if tg in (0, 2) else xnt_all[c]))
                    Wl("tensor", [p2f[g]])
                    cp = None
                    for dtl in range(16):
                        cp = OP("tensor", lambda t, _p=pp[:], _w=apb[:, slot * 2048 + dtl * 128:
                                slot * 2048 + (dtl + 1) * 128],
                                _x=xnt[:, dtl * 2048 + tg * 512:dtl * 2048 + (tg + 1) * 512],
                                st=(dtl == 0), sp=(dtl == 15): t.matmul(_p, _w, _x, start=st, stop=sp),
                                incs="pe" if dtl == 15 else None) or cp
                    if tg == 3:
                        apb_use[slot] = cp
                    if ft < 2:
                        dst = uni[:, U_QT + ft * 2048 + tg * 512:U_QT + ft * 2048 + (tg + 1) * 512]
                    elif ft < 4:
                        dst = uni[:, U_KT + (ft - 2) * 2048 + tg * 512:U_KT + (ft - 2) * 2048 + (tg + 1) * 512]
                    else:
                        dst = uni[:, U_VT + (ft - 4) * 2048 + tg * 512:U_VT + (ft - 4) * 2048 + (tg + 1) * 512]
                    ev = OP("scalar", lambda s, o=dst, i2=pp[:], bi=bqkv_sb[:, ft:ft + 1]:
                            s.activation(out=o, in_=i2, func=AF.Identity, bias=bi),
                            incs="act", waits=[cp])
                    p2f[g] = ev
                    last_ev = ev
            qk_ev[0] = last_ev

        def emit_B3(c):
            Wl("tensor", [qk_ev[0]])
            for h in range(2):
                for half in range(2):
                    pd = P3[half]
                    Wl("tensor", [p3f[half]])
                    ptr = None
                    for k in range(8):
                        tk = half * 8 + k
                        ptr = OP("tensor", lambda t, p=pd[:, k * 128:(k + 1) * 128],
                                 i2=uni[:, U_VT + h * 2048 + tk * 128:U_VT + h * 2048 + (tk + 1) * 128]:
                                 t.transpose(p, i2, id_sb[:]),
                                 incs="pe" if k == 7 else None) or ptr
                    dstv = uni[:, U_V + half * 2048:U_V + (half + 1) * 2048].rearrange(
                        "p (a r) -> p a r", r=256)[:, :, h * 128:(h + 1) * 128]
                    ev = OP("scalar", lambda s, o=dstv,
                            i2=pd[:].rearrange("p (a r) -> p a r", r=128):
                            s.copy(out=o, in_=i2), incs="act", waits=[ptr])
                    p3f[half] = ev

        def emit_B4(c, h, avd):
            prev_nm = Z
            ptev = [Z, Z]
            prev_ea = prev_eb = Z
            for q in range(16):
                if q == 0:
                    Wl("tensor", [qk_ev[0]])
                    for k in range(4):
                        Wl("tensor", p1f[k])
                sc = [None, None]
                for kt in range(4):
                    if q > 0 and kt in (0, 2):
                        Wl("tensor", [prev_ea if kt == 0 else prev_eb] +
                           ([prev_nm] if kt == 0 else []))
                    r_ = OP("tensor", lambda t, _p=P1[:, kt * 512:(kt + 1) * 512],
                            _q=uni[:, U_QT + h * 2048 + q * 128:U_QT + h * 2048 + (q + 1) * 128],
                            _k=uni[:, U_KT + h * 2048 + kt * 512:U_KT + h * 2048 + (kt + 1) * 512]:
                            t.matmul(_p, _q, _k, start=True, stop=True),
                            incs="pe" if kt in (1, 3) else None)
                    if kt == 1:
                        sc[0] = r_
                    elif kt == 3:
                        sc[1] = r_
                rmx = OP("vector", lambda v: v.reduce_max(smal[:, 6:7], P1[:, 0:512], axis=AX.X,
                         negate=True), incs="dve", waits=[sc[0], prev_ea])
                nm = OP("vector", lambda v: v.tensor_single_scalar(smal[:, 7:8], smal[:, 6:7],
                        ISQ, A.mult), incs="dve", waits=[rmx])
                ea = OP("scalar", lambda s: s.activation(out=uni[:, U_P:U_P + 1024],
                        in_=P1[:, 0:1024], func=AF.Exp, scale=ISQ, bias=smal[:, 7:8],
                        accum_out=smal[:, 8:9]), incs="act", waits=[sc[0], nm])
                eb = OP("scalar", lambda s: s.activation(out=uni[:, U_P + 1024:U_P + 2048],
                        in_=P1[:, 1024:2048], func=AF.Exp, scale=ISQ, bias=smal[:, 7:8],
                        accum_out=smal[:, 14:15]), incs="act", waits=[sc[1]])
                sad = OP("vector", lambda v: v.tensor_tensor(smal[:, 9:10], smal[:, 8:9],
                         smal[:, 14:15], A.add), incs="dve", waits=[eb])
                rcp = OP("vector", lambda v: v.reciprocal(smal[:, 9:10], smal[:, 9:10]),
                         incs="dve", waits=[sad])
                pn = OP("vector", lambda v: v.tensor_single_scalar(uni[:, U_PN:U_PN + 2048],
                        uni[:, U_P:U_P + 2048], smal[:, 9:10], A.mult), incs="dve",
                        waits=[rcp])
                prev_ea, prev_eb, prev_nm = ea, eb, nm
                ql = q % 4
                Wl("tensor", [pn])
                for half in range(2):
                    pd = P3[half]
                    Wl("tensor", [p3f[half]])
                    ptr = None
                    for k in range(8):
                        kt = half * 8 + k
                        ptr = OP("tensor", lambda t, p=pd[:, k * 128:(k + 1) * 128],
                                 i2=uni[:, U_PN + kt * 128:U_PN + (kt + 1) * 128]:
                                 t.transpose(p, i2, id_sb[:]),
                                 incs="pe" if k == 7 else None) or ptr
                    dstv = uni[:, U_PT + half * 8 * 512:U_PT + (half * 8 + 8) * 512].rearrange(
                        "p (a r) -> p a r", r=512)[:, :, ql * 128:(ql + 1) * 128]
                    if half == 0:
                        ev = OP("vector", lambda v, o=dstv,
                                i2=pd[:].rearrange("p (a r) -> p a r", r=128):
                                v.tensor_copy(o, i2), incs="dve",
                                waits=[ptr, avd])
                    else:
                        ev = OP("scalar", lambda s, o=dstv,
                                i2=pd[:].rearrange("p (a r) -> p a r", r=128):
                                s.copy(out=o, in_=i2), incs="act", waits=[ptr])
                    p3f[half] = ev
                    ptev[half] = ev
                if ql == 3:
                    g4 = q // 4
                    gs = g4 % 2
                    Wl("tensor", ptev + [p2f[gs]])
                    for kt in range(16):
                        r_ = OP("tensor", lambda t, _p=P2[gs][:],
                                _v=uni[:, U_V + kt * 256 + h * 128:U_V + kt * 256 + (h + 1) * 128],
                                _pt=uni[:, U_PT + kt * 512:U_PT + (kt + 1) * 512],
                                st=(kt == 0), sp=(kt == 15): t.matmul(_p, _v, _pt, start=st, stop=sp),
                                incs="pe" if kt == 15 else None)
                        avd = r_ or avd
                    ev = OP("scalar", lambda s, o=ao[:, h * 2048 + g4 * 512:h * 2048 + (g4 + 1) * 512],
                            i2=P2[gs][:]: s.copy(out=o, in_=i2), incs="act", waits=[avd])
                    p2f[gs] = ev
                    ao_ev[0] = ev
            p1f[0] = [prev_ea, prev_nm]
            p1f[1] = [prev_ea]
            p1f[2] = [prev_eb]
            p1f[3] = [prev_eb]
            return avd

        def emit_B5(c, avd, dec_per_j=0):
            first = True
            for j in range(16):
                if dec_per_j:
                    emit_decode(dec_per_j)
                ldg = DMA("sync", wpool[:, (j % 2) * 2048:(j % 2 + 1) * 2048], dec["g"][j],
                          waits=[dec_store[("g", j)], wslot_use[j % 2]])
                ldu = DMA("sync", wpool[:, (2 + j % 2) * 2048:(3 + j % 2) * 2048], dec["u"][j],
                          waits=[dec_store[("u", j)], wslot_use[2 + j % 2]])
                for tg in range(4):
                    qb = tg % 2
                    bg = P1[:, (2 * qb) * 512:(2 * qb + 1) * 512]
                    bu = P1[:, (2 * qb + 1) * 512:(2 * qb + 2) * 512]
                    Wl("tensor", [ldg, ldu] + xnt_all[c])
                    Wl("tensor", p1f[2 * qb] + p1f[2 * qb + 1])
                    gmm = umm = None
                    for kd in range(16):
                        gmm = OP("tensor", lambda t, _p=bg, _w=wpool[:, (j % 2) * 2048 + kd * 128:
                                 (j % 2) * 2048 + (kd + 1) * 128],
                                 _x=xnt[:, kd * 2048 + tg * 512:kd * 2048 + (tg + 1) * 512],
                                 st=(kd == 0), sp=(kd == 15): t.matmul(_p, _w, _x, start=st, stop=sp),
                                 incs="pe" if kd == 15 else None) or gmm
                    for kd in range(16):
                        umm = OP("tensor", lambda t, _p=bu, _w=wpool[:, (2 + j % 2) * 2048 + kd * 128:
                                 (2 + j % 2) * 2048 + (kd + 1) * 128],
                                 _x=xnt[:, kd * 2048 + tg * 512:kd * 2048 + (tg + 1) * 512],
                                 st=(kd == 0), sp=(kd == 15): t.matmul(_p, _w, _x, start=st, stop=sp),
                                 incs="pe" if kd == 15 else None) or umm
                    if tg == 3:
                        wslot_use[j % 2] = gmm
                        wslot_use[2 + j % 2] = umm
                    geld = OP("scalar", lambda s, _g=bg, o=gg[:, qb * 512:(qb + 1) * 512]:
                              s.activation(out=o, in_=_g, func=AF.Gelu_apprx_tanh),
                              incs="act", waits=[gmm, gg_free[qb]])
                    p1f[2 * qb] = [geld]
                    iw = [umm, geld]
                    if first:
                        iw += [avd, ao_ev[0]]
                        first = False
                    intd = OP("vector", lambda v, _u=bu,
                              o=uni[:, j * 2048 + tg * 512:j * 2048 + (tg + 1) * 512],
                              g2=gg[:, qb * 512:(qb + 1) * 512]:
                              v.tensor_tensor(o, g2, _u, A.mult), incs="dve", waits=iw)
                    p1f[2 * qb + 1] = [intd]
                    gg_free[qb] = intd
                while dec_pend:
                    emit_decode_pe()
            b5_last[c] = cnt["pe"]
            b5_intd[0] = cnt["dve"]

        def emit_B6(c):
            W("tensor", "act", ao_ev[0][1])
            W("tensor", "dve", b5_intd[0])
            if c == 0:
                W("tensor", "dve", a1_all)
            wo_toks = []
            for kt in range(2):
                wo_toks.append(DMA("gpsimd", wpool[:, (2 + kt) * 2048:(3 + kt) * 2048],
                                   dec_a[6 + kt], waits=deca_st[6 + kt] + [wslot_use[2 + kt]]))
            part_last = None
            rs_half = [None, None]
            for idx, dtl in enumerate(list(range(0, 16, 2)) + list(range(1, 16, 2))):
                slot = idx % 2
                ldd = DMA("gpsimd", wpool[:, slot * 2048:(slot + 1) * 2048],
                          dec["d"][dtl],
                          waits=[dec_store[("d", dtl)], wslot_use[slot]])
                for tg in range(4):
                    gs = (4 * dtl + tg) % 2
                    pp = P2[gs]
                    Wl("tensor", [ldd] + wo_toks)
                    Wl("tensor", [p2f[gs]])
                    for kt in range(2):
                        OP("tensor", lambda t, _p=pp[:],
                           _w=wpool[:, (2 + kt) * 2048 + dtl * 128:(2 + kt) * 2048 + (dtl + 1) * 128],
                           _a=ao[:, kt * 2048 + tg * 512:kt * 2048 + (tg + 1) * 512],
                           st=(kt == 0): t.matmul(_p, _w, _a, start=st, stop=False))
                    pdn = None
                    for kn in range(16):
                        pdn = OP("tensor", lambda t, _p=pp[:],
                                 _w=wpool[:, slot * 2048 + kn * 128:slot * 2048 + (kn + 1) * 128],
                                 _i=uni[:, kn * 2048 + tg * 512:kn * 2048 + (tg + 1) * 512],
                                 sp=(kn == 15): t.matmul(_p, _w, _i, start=False, stop=sp),
                                 incs="pe" if kn == 15 else None) or pdn
                    if tg == 3:
                        wslot_use[slot] = pdn
                        if dtl == 15:
                            wslot_use[2] = pdn
                            wslot_use[3] = pdn
                    ypc = OP("scalar", lambda s, o=yp[:, gs * 512:(gs + 1) * 512], i2=pp[:]:
                             s.copy(out=o, in_=i2), incs="act",
                             waits=[pdn, yp_free[gs]])
                    p2f[gs] = ypc
                    part_last = DMA("scalar",
                                    part[c][dtl % 2][(dtl // 2) * 128:(dtl // 2 + 1) * 128,
                                                     tg * 512:(tg + 1) * 512],
                                    yp[:, gs * 512:(gs + 1) * 512], waits=[ypc])
                    yp_free[gs] = part_last
                if idx in (7, 15):
                    h = idx // 8
                    Wl("gpsimd", qmark("scalar"))
                    rs_half[h] = OP("gpsimd", lambda g, _c=c, _h=h: g.collective_compute(
                        "ReduceScatter", A.add, replica_groups=[list(range(R))],
                        ins=[part[_c][_h].opt()], outs=[rs_o[_c][_h].opt()]), incs="cc")
            rs_done[c] = rs_half

        def emit_F(c):
            seen = set()
            stv = None
            for pc in range(8):
                dtl, tg = pc // 4, pc % 4
                slot = pc % 2
                rsb = fnb[:, slot * 1536:slot * 1536 + 512]
                xtb = fnb[:, slot * 1536 + 512:slot * 1536 + 1024]
                yb = fnb[:, slot * 1536 + 1024:slot * 1536 + 1536]
                firstw = [("cc", rs_done[c][dtl][1])] if dtl not in seen else []
                seen.add(dtl)
                l1 = DMA("gpsimd", rsb,
                         rs_o[c][dtl][0:128, tg * 512:(tg + 1) * 512],
                         waits=firstw + [fn_free[slot]])
                l2 = DMA("gpsimd", xtb,
                         xT_s[dtl * 128:(dtl + 1) * 128, c * 2048 + tg * 512:c * 2048 + (tg + 1) * 512])
                yd = OP("vector", lambda v, _r=rsb, _x=xtb, _y=yb, bi=bo_sb[:, dtl:dtl + 1]:
                        v.scalar_tensor_tensor(_y, _r, bi, _x, A.add, A.add),
                        incs="dve", waits=[l1, l2])
                stv = DMA("gpsimd",
                          yT_s[dtl * 128:(dtl + 1) * 128, c * 2048 + tg * 512:c * 2048 + (tg + 1) * 512],
                          yb, waits=[yd])
                fn_free[slot] = yd
            return stv

        for pi in range(4):
            emit_decode(6)
            emit_A1(pi)
        emit_decode(8)
        a1_all = cnt["dve"]
        a1_st = qmark("scalar")
        emit_xnt_load(0)
        emit_B2(0)
        emit_B3(0)
        avd = emit_B4(0, 0, Z)
        avd = emit_B4(0, 1, avd)
        emit_B5(0, avd, dec_per_j=1)
        emit_B6(0)
        for c in range(1, 4):
            emit_xnt_load(c)
            emit_B2(c)
            emit_B3(c)
            avd = emit_B4(c, 0, avd)
            avd = emit_B4(c, 1, avd)
            emit_B5(c, avd)
            emit_B6(c)
            emit_F(c - 1)
        last_st = emit_F(3)
        Wl("gpsimd", [last_st])
        Wl("gpsimd", qmark("gpsimd"))

        with nc.Block() as block:
            @block.sync
            def _(e):
                for f in ops["sync"]:
                    f(e)

            @block.vector
            def _(e):
                for f in ops["vector"]:
                    f(e)

            @block.scalar
            def _(e):
                for f in ops["scalar"]:
                    f(e)

            @block.tensor
            def _(e):
                for f in ops["tensor"]:
                    f(e)

            @block.gpsimd
            def _(e):
                for f in ops["gpsimd"]:
                    f(e)
    return nc


_CACHE = {}


def kernel(L0_attn_x, L0_attn_wq_codes, L0_attn_wq_scales, L0_attn_bq,
           L0_attn_wk_codes, L0_attn_wk_scales, L0_attn_bk,
           L0_attn_wv_codes, L0_attn_wv_scales, L0_attn_bv,
           L0_attn_wo_codes, L0_attn_wo_scales, L0_attn_bo,
           L0_ffn_gate_W_packed, L0_ffn_gate_scales,
           L0_ffn_up_W_packed, L0_ffn_up_scales,
           L0_ffn_down_W_packed, L0_ffn_down_scales):
    if "nc" not in _CACHE:
        _CACHE["nc"] = build_nc()
    nc = _CACHE["nc"]

    ac = np.ascontiguousarray
    xf = np.asarray(L0_attn_x, dtype=np.float32).reshape(T, D)
    xT = ac(xf.T)
    rep = np.repeat(np.arange(64), 32)
    wq_se_f = np.asarray(L0_attn_wq_scales)[rep, :]
    wk_se_f = np.asarray(L0_attn_wk_scales)[rep, :]
    wv_se_f = np.asarray(L0_attn_wv_scales)[rep, :]
    wo_se_f = np.asarray(L0_attn_wo_scales)[rep, :]
    ident = np.eye(128, dtype=ml_dtypes.bfloat16)
    bq = np.asarray(L0_attn_bq)
    bk = np.asarray(L0_attn_bk)
    bv = np.asarray(L0_attn_bv)
    bo = np.asarray(L0_attn_bo)

    in_maps = []
    for c in range(R):
        f0 = FT * c
        n0 = 2048 * c
        m = {
            "x_tok": ac(xf[TKC * c:TKC * (c + 1)]),
            "xT_s": ac(xT[f0:f0 + FT]),
            "wq_c": ac(np.asarray(L0_attn_wq_codes)[:, f0:f0 + FT]),
            "wk_c": ac(np.asarray(L0_attn_wk_codes)[:, f0:f0 + FT]),
            "wv_c": ac(np.asarray(L0_attn_wv_codes)[:, f0:f0 + FT]),
            "wq_se": ac(wq_se_f[:, f0:f0 + FT]),
            "wk_se": ac(wk_se_f[:, f0:f0 + FT]),
            "wv_se": ac(wv_se_f[:, f0:f0 + FT]),
            "wo_c": ac(np.asarray(L0_attn_wo_codes)[f0:f0 + FT, :]),
            "wo_se": ac(wo_se_f[f0:f0 + FT, :]),
            "bqkv": ac(np.stack([bq[f0:f0 + 128], bq[f0 + 128:f0 + 256],
                                 bk[f0:f0 + 128], bk[f0 + 128:f0 + 256],
                                 bv[f0:f0 + 128], bv[f0 + 128:f0 + 256]], axis=1)),
            "bo_s": ac(np.stack([bo[f0:f0 + 128], bo[f0 + 128:f0 + 256]], axis=1)),
            "g_p": ac(np.asarray(L0_ffn_gate_W_packed)[n0:n0 + 2048]),
            "g_s": ac(np.asarray(L0_ffn_gate_scales)[n0:n0 + 2048]),
            "u_p": ac(np.asarray(L0_ffn_up_W_packed)[n0:n0 + 2048]),
            "u_s": ac(np.asarray(L0_ffn_up_scales)[n0:n0 + 2048]),
            "d_p": ac(np.asarray(L0_ffn_down_W_packed)[:, 1024 * c:1024 * (c + 1)]),
            "d_s": ac(np.asarray(L0_ffn_down_scales)[:, 64 * c:64 * (c + 1)]),
            "ident": ident,
        }
        in_maps.append(m)

    res = run_bass_kernel_spmd(nc, in_maps, core_ids=list(range(R)))
    shards = [np.asarray(res.results[c]["yT_s"]) for c in range(R)]
    yT = np.concatenate(shards, axis=0)
    return ac(yT.T).reshape(B_, S, D)


# version 25
# speedup vs baseline: 1.1189x; 1.0453x over previous
import numpy as np
import ml_dtypes
from contextlib import ExitStack
import concourse.bass as bass
import concourse.mybir as mybir
from concourse.bass_utils import run_bass_kernel_spmd

F32 = mybir.dt.float32
BF16 = mybir.dt.bfloat16
I32 = mybir.dt.int32
U16 = mybir.dt.uint16
A = mybir.AluOpType
AF = mybir.ActivationFunctionType
AX = mybir.AxisListType

R = 8
B_, S, D = 4, 2048, 2048
T = B_ * S
TKC = T // R
FT = 256
ISQ = 0.08838834764831845
NQS = 4


def build_nc():
    nc = bass.Bass("TRN2", target_bir_lowering=False)

    def din(name, shape, dt):
        return nc.dram_tensor(name, shape, dt, kind="ExternalInput").ap()

    def dint(name, shape, dt):
        return nc.dram_tensor(name, shape, dt, kind="Internal").ap()

    x_tok = din("x_tok", [TKC, D], F32)
    xT_s = din("xT_s", [FT, T], F32)
    w_qkv_c = [din(f"w{p}_c", [D, FT], F32) for p in "qkv"]
    w_qkv_se = [din(f"w{p}_se", [D, FT], F32) for p in "qkv"]
    wo_c = din("wo_c", [FT, D], F32)
    wo_se = din("wo_se", [FT, D], F32)
    bqkv_h = din("bqkv", [128, 6], F32)
    bo_h = din("bo_s", [128, 2], F32)
    ffn_p = {w: din(f"{w}_p", [2048, 1024], I32) for w in ("g", "u", "d")}
    ffn_s = {w: din(f"{w}_s", [2048, 64], F32) for w in ("g", "u", "d")}
    ident_h = din("ident", [128, 128], BF16)
    yT_s = nc.dram_tensor("yT_s", [FT, T], F32, kind="ExternalOutput").ap()

    ag_in = [dint(f"ag_in{h}", [D, 512], BF16) for h in range(2)]
    ag_out = [dint(f"ag_out{h}", [R * D, 512], BF16) for h in range(2)]
    dec = {w: dint(f"dec_{w}", [16, 128, 2048], BF16) for w in ("g", "u", "d")}
    dec_a = dint("dec_a", [8, 128, 2048], BF16)
    part = [[dint(f"part{c}h{h}", [D // 2, S], F32) for h in range(2)] for c in range(4)]
    rs_o = [[dint(f"rs_o{c}h{h}", [128, S], F32) for h in range(2)] for c in range(4)]

    with ExitStack() as ctx:
        def sbuf(name, shape, dt):
            return ctx.enter_context(nc.sbuf_tensor(name, shape, dt))

        def psum(name, shape, dt):
            return ctx.enter_context(nc.psum_tensor(name, shape, dt))

        xnt = sbuf("xnt", [128, 32768], BF16)
        uni = sbuf("uni", [128, 32768], BF16)
        ao = sbuf("ao", [128, 4096], BF16)
        wpool = sbuf("wpool", [128, 8192], BF16)
        apb = sbuf("apb", [128, 4096], BF16)
        pk = sbuf("pk", [128, 2048], I32)
        dtmp = sbuf("dtmp", [128, 4096], I32)
        codes = sbuf("codes", [128, 2048], U16)
        dq = sbuf("dq", [128, 2048], BF16)
        stage = sbuf("stage", [128, 4096], BF16)
        scb = sbuf("scb", [128, 128], F32)
        id_sb = sbuf("id_sb", [128, 128], BF16)
        bqkv_sb = sbuf("bqkv_sb", [128, 6], F32)
        bo_sb = sbuf("bo_sb", [128, 2], F32)
        gg = sbuf("gg", [128, 1024], BF16)
        yp = sbuf("yp", [128, 1024], F32)
        smal = sbuf("smal", [128, 16], F32)
        xn_t = None

        P1 = psum("P1", [128, 2048], F32)
        P2 = [psum("P2a", [128, 512], F32), psum("P2b", [128, 512], F32)]
        P3 = [psum("P3a", [128, 1024], BF16), psum("P3b", [128, 1024], BF16)]

        qsems = {q: [f"{q}{i}" for i in range(NQS)] for q in ("sync", "scalar", "gpsimd")}
        semnames = ["cc", "pe", "dve", "act"] + sum(qsems.values(), [])
        sem = {n: ctx.enter_context(nc.semaphore(n)) for n in semnames}
        cnt = {n: 0 for n in semnames}
        qrot = {q: 0 for q in qsems}
        ops = {"sync": [], "vector": [], "scalar": [], "tensor": [], "gpsimd": []}

        def em(eng, fn):
            ops[eng].append(fn)

        def W(eng, s, v):
            if v > 0:
                em(eng, lambda e, _s=sem[s], _v=v: e.wait_ge(_s, _v))

        def Wl(eng, toks):
            for (s, v) in toks:
                W(eng, s, v)

        def inc(s, n=1):
            cnt[s] += n
            return cnt[s]

        def DMA(q, out_ap, in_ap, waits=()):
            Wl(q, waits)
            sname = qsems[q][qrot[q] % NQS]
            qrot[q] += 1
            c = inc(sname, 16)
            em(q, lambda e, o=out_ap, i=in_ap, sm=sem[sname]:
               e.dma_start(out=o, in_=i).then_inc(sm, 16))
            return (sname, c)

        def qmark(q):
            return [(s, cnt[s]) for s in qsems[q] if cnt[s] > 0]

        def OP(eng, build, incs=None, waits=()):
            Wl(eng, waits)
            if incs:
                c = inc(incs)
                em(eng, lambda e, b=build, sm=sem[incs]: b(e).then_inc(sm, 1))
                return (incs, c)
            em(eng, lambda e, b=build: b(e))
            return None

        u_f32 = uni[:].bitcast(F32)
        wp_f32 = wpool[:].bitcast(F32)
        jnk = codes[:].bitcast(BF16)
        fnb = dtmp[:].bitcast(F32)

        xn_t = stage[:, 0:2048]
        U_QT, U_KT, U_V, U_P, U_PN, U_VT, U_PT = 0, 4096, 8192, 12288, 14336, 16384, 16384
        Z = ("pe", 0)

        p1f = [[Z], [Z], [Z], [Z]]
        p2f = [Z, Z]
        p3f = [Z, Z]

        for o, i in ((id_sb, ident_h), (bqkv_sb, bqkv_h), (bo_sb, bo_h)):
            DMA("sync", o[:], i)
        con_ld = qmark("sync")

        OP("vector", lambda v: v.memset(smal[:, 10:11], 1e-5))
        OP("vector", lambda v: v.memset(smal[:, 11:12], 16128.0), incs="dve")
        ln_done = []
        ag_done = [None, None]
        pe_tr = Z
        for tt in range(8):
            slot = tt % 2
            xsl = wp_f32[:, slot * 2048:(slot + 1) * 2048]
            ldw = [ln_done[tt - 2]] if tt >= 2 else ()
            ldv = DMA("sync", xsl, x_tok[tt * 128:(tt + 1) * 128, :], waits=ldw)
            rs1 = OP("vector", lambda v, x=xsl: v.reduce_sum(smal[:, 0:1], x, axis=AX.X),
                     incs="dve", waits=[ldv])
            nmn = OP("vector", lambda v: v.tensor_single_scalar(smal[:, 2:3], smal[:, 0:1],
                     -1.0 / D, A.mult), incs="dve", waits=[rs1])
            sq = OP("scalar", lambda s, x=xsl: s.activation(out=jnk[:], in_=x, func=AF.Square,
                    accum_out=smal[:, 1:2]), incs="act", waits=[ldv])
            OP("vector", lambda v: v.tensor_single_scalar(smal[:, 3:4], smal[:, 1:2],
                                                          1.0 / D, A.mult), waits=[sq, nmn])
            m2c = OP("vector", lambda v: v.tensor_tensor(smal[:, 4:5], smal[:, 2:3],
                     smal[:, 2:3], A.mult), incs="dve")
            vd = OP("vector", lambda v: v.tensor_tensor(smal[:, 12:13], smal[:, 3:4],
                    smal[:, 4:5], A.subtract), incs="dve", waits=[m2c])
            sdc = OP("scalar", lambda s: s.activation(out=smal[:, 13:14], in_=smal[:, 12:13],
                     func=AF.Sqrt, bias=smal[:, 10:11], scale=1.0), incs="act", waits=[vd])
            rcc = OP("vector", lambda v: v.reciprocal(smal[:, 5:6], smal[:, 13:14]),
                     incs="dve", waits=[sdc])
            xw = [rcc] + ([pe_tr] if tt >= 1 else [])
            xnd = OP("vector", lambda v, x=xsl: v.tensor_scalar(xn_t[:], x, smal[:, 2:3],
                     smal[:, 5:6], A.add, A.mult), incs="dve", waits=xw)
            ln_done.append(xnd)
            Wl("tensor", [xnd])
            if tt == 0:
                Wl("tensor", con_ld)
            for half in range(2):
                pd = P3[half]
                Wl("tensor", [p3f[half]])
                for k in range(8):
                    dtl = half * 8 + k
                    pe_tr = OP("tensor", lambda t, p=pd[:, k * 128:(k + 1) * 128],
                               i=xn_t[:, dtl * 128:(dtl + 1) * 128]:
                               t.transpose(p, i, id_sb[:]),
                               incs="pe" if k == 7 else None) or pe_tr
                dst = uni[:].rearrange("p (dt q) -> p dt q", q=1024)[
                    :, half * 8:half * 8 + 8, tt * 128:(tt + 1) * 128]
                ev = OP("scalar", lambda s, o=dst,
                        i=pd[:].rearrange("p (dt q) -> p dt q", q=128):
                        s.copy(out=o, in_=i), incs="act", waits=[pe_tr])
                p3f[half] = ev
            if tt in (3, 7):
                th = tt // 4
                W("gpsimd", "act", cnt["act"])
                for dtl in range(16):
                    DMA("gpsimd", ag_in[th][dtl * 128:(dtl + 1) * 128, :],
                        uni[:, dtl * 1024 + th * 512:dtl * 1024 + th * 512 + 512])
                Wl("gpsimd", qmark("gpsimd"))
                ag_done[th] = OP("gpsimd", lambda g, _h=th: g.collective_compute(
                    "AllGather", A.bypass, replica_groups=[list(range(R))],
                    ins=[ag_in[_h].opt()], outs=[ag_out[_h].opt()]), incs="cc")
        ag_st = qmark("gpsimd")

        st_free2 = [("act", 0), ("act", 0), ("act", 0), ("act", 0)]
        a1_srcs = [(w_qkv_c[0], w_qkv_se[0]), (w_qkv_c[1], w_qkv_se[1]),
                   (w_qkv_c[2], w_qkv_se[2]), (wo_c, wo_se)]
        a1_ttp = [None] * 4
        deca_st = {}
        a1_state = {"sidx": 0}

        def emit_A1(pi):
            sidx = a1_state["sidx"]
            ch, sh = a1_srcs[pi]
            base = (pi % 2) * 8192
            lw = list(ag_st) if pi < 2 else [a1_ttp[pi - 2]]
            toks = []
            if pi < 3:
                cin = ch[:].rearrange("(dt p) f -> p dt f", p=128)
                sin = sh[:].rearrange("(dt p) f -> p dt f", p=128)
                toks.append(DMA("scalar",
                                u_f32[:, base:base + 4096].rearrange("p (dt f) -> p dt f", f=FT),
                                cin, waits=lw))
                toks.append(DMA("scalar",
                                u_f32[:, base + 4096:base + 8192].rearrange("p (dt f) -> p dt f", f=FT),
                                sin))
            else:
                toks.append(DMA("scalar", u_f32[:, base:base + 2048], ch[0:128, :], waits=lw))
                toks.append(DMA("scalar", u_f32[:, base + 2048:base + 4096], ch[128:256, :]))
                toks.append(DMA("scalar", u_f32[:, base + 4096:base + 6144], sh[0:128, :]))
                toks.append(DMA("scalar", u_f32[:, base + 6144:base + 8192], sh[128:256, :]))
            Wl("vector", toks)
            a1_tt = None
            for f in range(2):
                for hg in range(2):
                    ss = sidx % 4
                    sidx += 1
                    Wl("vector", [st_free2[ss]])
                    for k in range(8):
                        dtl = hg * 8 + k
                        if pi < 3:
                            a_ = u_f32[:, base + dtl * 256 + f * 128:base + dtl * 256 + f * 128 + 128]
                            b_ = u_f32[:, base + 4096 + dtl * 256 + f * 128:base + 4096 + dtl * 256 + f * 128 + 128]
                        else:
                            a_ = u_f32[:, base + f * 2048 + dtl * 128:base + f * 2048 + (dtl + 1) * 128]
                            b_ = u_f32[:, base + 4096 + f * 2048 + dtl * 128:base + 4096 + f * 2048 + (dtl + 1) * 128]
                        o = stage[:, ss * 1024 + k * 128:ss * 1024 + (k + 1) * 128]
                        a1_tt = OP("vector", lambda v, _o=o, _a=a_, _b=b_:
                                   v.tensor_tensor(_o, _a, _b, A.mult),
                                   incs="dve" if k == 7 else None) or a1_tt
                    stv = DMA("scalar", dec_a[2 * pi + f][:, hg * 1024:(hg + 1) * 1024],
                              stage[:, ss * 1024:(ss + 1) * 1024], waits=[a1_tt])
                    st_free2[ss] = stv
                    deca_st.setdefault(2 * pi + f, []).append(stv)
            a1_ttp[pi] = a1_tt
            a1_state["sidx"] = sidx

        Tt = [dtmp[:, i * 1024:(i + 1) * 1024] for i in range(4)]
        pk_free = [("dve", 0), ("dve", 0)]
        sc_free = [("dve", 0), ("dve", 0)]
        st_free = st_free2
        dec_store = {}
        dec_ptr = {}
        dec_i = [0]
        dec_pend = []
        dec_order = [(w, rt) for rt in range(16) for w in ("g", "u")] + \
                    [("d", rt) for rt in range(16)]
        p0_act = cnt["act"]

        def emit_decode_dve(n):
            for _ in range(n):
                if dec_i[0] >= len(dec_order):
                    return
                i = dec_i[0]
                dec_i[0] += 1
                w, rt = dec_order[i]
                slot = i % 2
                pks = pk[:, slot * 1024:(slot + 1) * 1024]
                scs = scb[:, slot * 64:(slot + 1) * 64]
                pkv = DMA("gpsimd", pks, ffn_p[w][rt * 128:(rt + 1) * 128, :],
                          waits=[pk_free[slot]])
                scv = DMA("gpsimd", scs, ffn_s[w][rt * 128:(rt + 1) * 128, :],
                          waits=[sc_free[slot]])
                w0 = [pkv] + ([("act", p0_act)] if i == 0 else [])
                dcd = None
                for lohi in range(2):
                    if lohi == 0:
                        OP("vector", lambda v, p=pks: v.tensor_single_scalar(
                           Tt[0][:], p, 7, A.bitwise_and), waits=w0)
                        OP("vector", lambda v, p=pks: v.tensor_single_scalar(
                           Tt[1][:], p, 8, A.bitwise_and))
                    else:
                        OP("vector", lambda v, p=pks: v.tensor_scalar(
                           Tt[0][:], p, 4, 7, A.logical_shift_right, A.bitwise_and))
                        pkfree = OP("vector", lambda v, p=pks: v.tensor_scalar(
                                    Tt[1][:], p, 4, 8, A.logical_shift_right, A.bitwise_and),
                                    incs="dve")
                        pk_free[slot] = pkfree
                    bbv = OP("vector", lambda v: v.tensor_scalar(Tt[2][:], Tt[0][:],
                             64, 16128, A.mult, A.add), incs="dve")
                    mmn = OP("vector", lambda v: v.scalar_tensor_tensor(Tt[3][:], Tt[0][:],
                             16128.0, Tt[2][:], A.mult, A.min), incs="dve", waits=[bbv])
                    dst = codes[:, lohi:2048:2]
                    dcd = OP("vector", lambda v, _d=dst: v.scalar_tensor_tensor(
                             _d, Tt[1][:], 4096.0, Tt[3][:], A.mult, A.add),
                             incs="dve", waits=[mmn])
                dqd = OP("vector", lambda v, _s=scs:
                         v.tensor_tensor(dq[:].rearrange("p (j r) -> p j r", r=32),
                                         codes[:].bitcast(BF16).rearrange("p (j r) -> p j r", r=32),
                                         _s.unsqueeze(2).broadcast_to([128, 64, 32]),
                                         A.mult), incs="dve",
                         waits=[scv, dcd, dec_ptr.get(i - 1, Z)])
                sc_free[slot] = dqd
                dec_pend.append((i, w, rt, dqd))

        def emit_decode_pe():
            i, w, rt, dqd = dec_pend.pop(0)
            Wl("tensor", [dqd])
            ptr = None
            for half in range(2):
                pd = P3[half]
                Wl("tensor", [p3f[half]])
                for k in range(8):
                    kt2 = half * 8 + k
                    ptr = OP("tensor", lambda t, p=pd[:, k * 128:(k + 1) * 128],
                             i2=dq[:, kt2 * 128:(kt2 + 1) * 128]:
                             t.transpose(p, i2, id_sb[:]),
                             incs="pe" if k == 7 else None) or ptr
                ss2 = (2 * i + half) % 4
                ev = OP("scalar", lambda s, o=stage[:, ss2 * 1024:(ss2 + 1) * 1024],
                        i2=pd[:]: s.copy(out=o, in_=i2), incs="act",
                        waits=[ptr, st_free[ss2]])
                p3f[half] = ev
                stv = DMA("scalar", dec[w][rt][:, half * 1024:(half + 1) * 1024],
                          stage[:, ss2 * 1024:(ss2 + 1) * 1024], waits=[ev])
                st_free[ss2] = stv
            dec_ptr[i] = ptr
            dec_store[(w, rt)] = stv

        def emit_decode(n):
            for _ in range(n):
                if dec_i[0] >= len(dec_order):
                    return
                emit_decode_dve(1)
                emit_decode_pe()

        def dec_slot():
            if dec_pend:
                emit_decode_pe()
            if dec_i[0] < len(dec_order):
                emit_decode_dve(1)

        xnt_h0 = [None] * 4
        xnt_all = [None] * 4
        apb_use = [Z, Z]
        wslot_use = [Z, Z, Z, Z]
        yp_free = [("act", 0), ("act", 0)]
        b5_last = [0, 0, 0, 0]
        rs_done = [0, 0, 0, 0]
        fn_free = [("dve", 0), ("dve", 0)]
        gg_free = [("dve", 0), ("dve", 0)]
        qk_ev = [Z]
        ao_ev = [Z]
        b5_intd = [0]

        def emit_xnt_load(c):
            w0 = [("pe", b5_last[c - 1])] if c > 0 else []
            for th in range(2):
                waits = w0 + ([("cc", ag_done[th][1])] if c == 0 else [])
                w0 = []
                for dtl in range(16):
                    for h2 in range(2):
                        DMA("sync",
                            xnt[:, dtl * 2048 + h2 * 1024 + th * 512:
                                dtl * 2048 + h2 * 1024 + th * 512 + 512],
                            ag_out[th][(2 * c + h2) * D + dtl * 128:
                                       (2 * c + h2) * D + (dtl + 1) * 128, :],
                            waits=waits)
                        waits = ()
                if th == 0:
                    xnt_h0[c] = qmark("sync")
            xnt_all[c] = qmark("sync")

        def emit_B2(c):
            last_ev = Z
            for ft in range(6):
                slot = ft % 2
                apb_ld = DMA("sync", apb[:, slot * 2048:(slot + 1) * 2048],
                             dec_a[ft], waits=deca_st[ft] + [apb_use[slot]])
                for tg in (0, 2, 1, 3):
                    g = (4 * ft + tg) % 2
                    pp = P2[g]
                    Wl("tensor", [apb_ld] + (xnt_h0[c] if tg in (0, 2) else xnt_all[c]))
                    Wl("tensor", [p2f[g]])
                    cp = None
                    for dtl in range(16):
                        cp = OP("tensor", lambda t, _p=pp[:], _w=apb[:, slot * 2048 + dtl * 128:
                                slot * 2048 + (dtl + 1) * 128],
                                _x=xnt[:, dtl * 2048 + tg * 512:dtl * 2048 + (tg + 1) * 512],
                                st=(dtl == 0), sp=(dtl == 15): t.matmul(_p, _w, _x, start=st, stop=sp),
                                incs="pe" if dtl == 15 else None) or cp
                    if tg == 3:
                        apb_use[slot] = cp
                    if ft < 2:
                        dst = uni[:, U_QT + ft * 2048 + tg * 512:U_QT + ft * 2048 + (tg + 1) * 512]
                    elif ft < 4:
                        dst = uni[:, U_KT + (ft - 2) * 2048 + tg * 512:U_KT + (ft - 2) * 2048 + (tg + 1) * 512]
                    else:
                        dst = uni[:, U_VT + (ft - 4) * 2048 + tg * 512:U_VT + (ft - 4) * 2048 + (tg + 1) * 512]
                    ev = OP("scalar", lambda s, o=dst, i2=pp[:], bi=bqkv_sb[:, ft:ft + 1]:
                            s.activation(out=o, in_=i2, func=AF.Identity, bias=bi),
                            incs="act", waits=[cp])
                    p2f[g] = ev
                    last_ev = ev
                    if c == 0:
                        dec_slot()
            qk_ev[0] = last_ev

        def emit_B3(c):
            Wl("tensor", [qk_ev[0]])
            for h in range(2):
                for half in range(2):
                    pd = P3[half]
                    Wl("tensor", [p3f[half]])
                    ptr = None
                    for k in range(8):
                        tk = half * 8 + k
                        ptr = OP("tensor", lambda t, p=pd[:, k * 128:(k + 1) * 128],
                                 i2=uni[:, U_VT + h * 2048 + tk * 128:U_VT + h * 2048 + (tk + 1) * 128]:
                                 t.transpose(p, i2, id_sb[:]),
                                 incs="pe" if k == 7 else None) or ptr
                    dstv = uni[:, U_V + half * 2048:U_V + (half + 1) * 2048].rearrange(
                        "p (a r) -> p a r", r=256)[:, :, h * 128:(h + 1) * 128]
                    ev = OP("scalar", lambda s, o=dstv,
                            i2=pd[:].rearrange("p (a r) -> p a r", r=128):
                            s.copy(out=o, in_=i2), incs="act", waits=[ptr])
                    p3f[half] = ev
                    if c == 0:
                        dec_slot()

        def emit_B4(c, h, avd):
            prev_nm = Z
            ptev = [Z, Z]
            prev_ea = prev_eb = Z
            for q in range(16):
                if q == 0:
                    Wl("tensor", [qk_ev[0]])
                    for k in range(4):
                        Wl("tensor", p1f[k])
                sc = [None, None]
                for kt in range(4):
                    if q > 0 and kt in (0, 2):
                        Wl("tensor", [prev_ea if kt == 0 else prev_eb] +
                           ([prev_nm] if kt == 0 else []))
                    r_ = OP("tensor", lambda t, _p=P1[:, kt * 512:(kt + 1) * 512],
                            _q=uni[:, U_QT + h * 2048 + q * 128:U_QT + h * 2048 + (q + 1) * 128],
                            _k=uni[:, U_KT + h * 2048 + kt * 512:U_KT + h * 2048 + (kt + 1) * 512]:
                            t.matmul(_p, _q, _k, start=True, stop=True),
                            incs="pe" if kt in (1, 3) else None)
                    if kt == 1:
                        sc[0] = r_
                    elif kt == 3:
                        sc[1] = r_
                rmx = OP("vector", lambda v: v.reduce_max(smal[:, 6:7], P1[:, 0:512], axis=AX.X,
                         negate=True), incs="dve", waits=[sc[0], prev_ea])
                nm = OP("vector", lambda v: v.tensor_single_scalar(smal[:, 7:8], smal[:, 6:7],
                        ISQ, A.mult), incs="dve", waits=[rmx])
                ea = OP("scalar", lambda s: s.activation(out=uni[:, U_P:U_P + 1024],
                        in_=P1[:, 0:1024], func=AF.Exp, scale=ISQ, bias=smal[:, 7:8],
                        accum_out=smal[:, 8:9]), incs="act", waits=[sc[0], nm])
                eb = OP("scalar", lambda s: s.activation(out=uni[:, U_P + 1024:U_P + 2048],
                        in_=P1[:, 1024:2048], func=AF.Exp, scale=ISQ, bias=smal[:, 7:8],
                        accum_out=smal[:, 14:15]), incs="act", waits=[sc[1]])
                sad = OP("vector", lambda v: v.tensor_tensor(smal[:, 9:10], smal[:, 8:9],
                         smal[:, 14:15], A.add), incs="dve", waits=[eb])
                rcp = OP("vector", lambda v: v.reciprocal(smal[:, 9:10], smal[:, 9:10]),
                         incs="dve", waits=[sad])
                pn = OP("vector", lambda v: v.tensor_single_scalar(uni[:, U_PN:U_PN + 2048],
                        uni[:, U_P:U_P + 2048], smal[:, 9:10], A.mult), incs="dve",
                        waits=[rcp])
                prev_ea, prev_eb, prev_nm = ea, eb, nm
                ql = q % 4
                Wl("tensor", [pn])
                for half in range(2):
                    pd = P3[half]
                    Wl("tensor", [p3f[half]])
                    ptr = None
                    for k in range(8):
                        kt = half * 8 + k
                        ptr = OP("tensor", lambda t, p=pd[:, k * 128:(k + 1) * 128],
                                 i2=uni[:, U_PN + kt * 128:U_PN + (kt + 1) * 128]:
                                 t.transpose(p, i2, id_sb[:]),
                                 incs="pe" if k == 7 else None) or ptr
                    dstv = uni[:, U_PT + half * 8 * 512:U_PT + (half * 8 + 8) * 512].rearrange(
                        "p (a r) -> p a r", r=512)[:, :, ql * 128:(ql + 1) * 128]
                    if half == 0:
                        ev = OP("vector", lambda v, o=dstv,
                                i2=pd[:].rearrange("p (a r) -> p a r", r=128):
                                v.tensor_copy(o, i2), incs="dve",
                                waits=[ptr, avd])
                    else:
                        ev = OP("scalar", lambda s, o=dstv,
                                i2=pd[:].rearrange("p (a r) -> p a r", r=128):
                                s.copy(out=o, in_=i2), incs="act", waits=[ptr])
                    p3f[half] = ev
                    ptev[half] = ev
                if ql == 3:
                    g4 = q // 4
                    gs = g4 % 2
                    Wl("tensor", ptev + [p2f[gs]])
                    for kt in range(16):
                        r_ = OP("tensor", lambda t, _p=P2[gs][:],
                                _v=uni[:, U_V + kt * 256 + h * 128:U_V + kt * 256 + (h + 1) * 128],
                                _pt=uni[:, U_PT + kt * 512:U_PT + (kt + 1) * 512],
                                st=(kt == 0), sp=(kt == 15): t.matmul(_p, _v, _pt, start=st, stop=sp),
                                incs="pe" if kt == 15 else None)
                        avd = r_ or avd
                    ev = OP("scalar", lambda s, o=ao[:, h * 2048 + g4 * 512:h * 2048 + (g4 + 1) * 512],
                            i2=P2[gs][:]: s.copy(out=o, in_=i2), incs="act", waits=[avd])
                    p2f[gs] = ev
                    ao_ev[0] = ev
            p1f[0] = [prev_ea, prev_nm]
            p1f[1] = [prev_ea]
            p1f[2] = [prev_eb]
            p1f[3] = [prev_eb]
            return avd

        def emit_B5(c, avd, dec_per_j=0):
            first = True
            for j in range(16):
                if dec_per_j:
                    emit_decode(dec_per_j)
                ldg = DMA("sync", wpool[:, (j % 2) * 2048:(j % 2 + 1) * 2048], dec["g"][j],
                          waits=[dec_store[("g", j)], wslot_use[j % 2]])
                ldu = DMA("sync", wpool[:, (2 + j % 2) * 2048:(3 + j % 2) * 2048], dec["u"][j],
                          waits=[dec_store[("u", j)], wslot_use[2 + j % 2]])
                for tg in range(4):
                    qb = tg % 2
                    bg = P1[:, (2 * qb) * 512:(2 * qb + 1) * 512]
                    bu = P1[:, (2 * qb + 1) * 512:(2 * qb + 2) * 512]
                    Wl("tensor", [ldg, ldu] + xnt_all[c])
                    Wl("tensor", p1f[2 * qb] + p1f[2 * qb + 1])
                    gmm = umm = None
                    for kd in range(16):
                        gmm = OP("tensor", lambda t, _p=bg, _w=wpool[:, (j % 2) * 2048 + kd * 128:
                                 (j % 2) * 2048 + (kd + 1) * 128],
                                 _x=xnt[:, kd * 2048 + tg * 512:kd * 2048 + (tg + 1) * 512],
                                 st=(kd == 0), sp=(kd == 15): t.matmul(_p, _w, _x, start=st, stop=sp),
                                 incs="pe" if kd == 15 else None) or gmm
                    for kd in range(16):
                        umm = OP("tensor", lambda t, _p=bu, _w=wpool[:, (2 + j % 2) * 2048 + kd * 128:
                                 (2 + j % 2) * 2048 + (kd + 1) * 128],
                                 _x=xnt[:, kd * 2048 + tg * 512:kd * 2048 + (tg + 1) * 512],
                                 st=(kd == 0), sp=(kd == 15): t.matmul(_p, _w, _x, start=st, stop=sp),
                                 incs="pe" if kd == 15 else None) or umm
                    if tg == 3:
                        wslot_use[j % 2] = gmm
                        wslot_use[2 + j % 2] = umm
                    geld = OP("scalar", lambda s, _g=bg, o=gg[:, qb * 512:(qb + 1) * 512]:
                              s.activation(out=o, in_=_g, func=AF.Gelu_apprx_tanh),
                              incs="act", waits=[gmm, gg_free[qb]])
                    p1f[2 * qb] = [geld]
                    iw = [umm, geld]
                    if first:
                        iw += [avd, ao_ev[0]]
                        first = False
                    intd = OP("vector", lambda v, _u=bu,
                              o=uni[:, j * 2048 + tg * 512:j * 2048 + (tg + 1) * 512],
                              g2=gg[:, qb * 512:(qb + 1) * 512]:
                              v.tensor_tensor(o, g2, _u, A.mult), incs="dve", waits=iw)
                    p1f[2 * qb + 1] = [intd]
                    gg_free[qb] = intd
            while dec_pend:
                emit_decode_pe()
            while dec_i[0] < len(dec_order) and dec_per_j:
                emit_decode_dve(1)
                emit_decode_pe()
            b5_last[c] = cnt["pe"]
            b5_intd[0] = cnt["dve"]

        def emit_B6(c):
            W("tensor", "act", ao_ev[0][1])
            W("tensor", "dve", b5_intd[0])
            if c == 0:
                W("tensor", "dve", a1_all)
            wo_toks = []
            for kt in range(2):
                wo_toks.append(DMA("gpsimd", wpool[:, (2 + kt) * 2048:(3 + kt) * 2048],
                                   dec_a[6 + kt], waits=deca_st[6 + kt] + [wslot_use[2 + kt]]))
            part_last = None
            rs_half = [None, None]
            for idx, dtl in enumerate(list(range(0, 16, 2)) + list(range(1, 16, 2))):
                slot = idx % 2
                ldd = DMA("gpsimd", wpool[:, slot * 2048:(slot + 1) * 2048],
                          dec["d"][dtl],
                          waits=[dec_store[("d", dtl)], wslot_use[slot]])
                for tg in range(4):
                    gs = (4 * dtl + tg) % 2
                    pp = P2[gs]
                    Wl("tensor", [ldd] + wo_toks)
                    Wl("tensor", [p2f[gs]])
                    for kt in range(2):
                        OP("tensor", lambda t, _p=pp[:],
                           _w=wpool[:, (2 + kt) * 2048 + dtl * 128:(2 + kt) * 2048 + (dtl + 1) * 128],
                           _a=ao[:, kt * 2048 + tg * 512:kt * 2048 + (tg + 1) * 512],
                           st=(kt == 0): t.matmul(_p, _w, _a, start=st, stop=False))
                    pdn = None
                    for kn in range(16):
                        pdn = OP("tensor", lambda t, _p=pp[:],
                                 _w=wpool[:, slot * 2048 + kn * 128:slot * 2048 + (kn + 1) * 128],
                                 _i=uni[:, kn * 2048 + tg * 512:kn * 2048 + (tg + 1) * 512],
                                 sp=(kn == 15): t.matmul(_p, _w, _i, start=False, stop=sp),
                                 incs="pe" if kn == 15 else None) or pdn
                    if tg == 3:
                        wslot_use[slot] = pdn
                        if dtl == 15:
                            wslot_use[2] = pdn
                            wslot_use[3] = pdn
                    ypc = OP("scalar", lambda s, o=yp[:, gs * 512:(gs + 1) * 512], i2=pp[:]:
                             s.copy(out=o, in_=i2), incs="act",
                             waits=[pdn, yp_free[gs]])
                    p2f[gs] = ypc
                    part_last = DMA("scalar",
                                    part[c][dtl % 2][(dtl // 2) * 128:(dtl // 2 + 1) * 128,
                                                     tg * 512:(tg + 1) * 512],
                                    yp[:, gs * 512:(gs + 1) * 512], waits=[ypc])
                    yp_free[gs] = part_last
                if idx in (7, 15):
                    h = idx // 8
                    Wl("gpsimd", qmark("scalar"))
                    rs_half[h] = OP("gpsimd", lambda g, _c=c, _h=h: g.collective_compute(
                        "ReduceScatter", A.add, replica_groups=[list(range(R))],
                        ins=[part[_c][_h].opt()], outs=[rs_o[_c][_h].opt()]), incs="cc")
            rs_done[c] = rs_half

        def emit_F(c):
            seen = set()
            stv = None
            for pc in range(8):
                dtl, tg = pc // 4, pc % 4
                slot = pc % 2
                rsb = fnb[:, slot * 1536:slot * 1536 + 512]
                xtb = fnb[:, slot * 1536 + 512:slot * 1536 + 1024]
                yb = fnb[:, slot * 1536 + 1024:slot * 1536 + 1536]
                firstw = [("cc", rs_done[c][dtl][1])] if dtl not in seen else []
                seen.add(dtl)
                l1 = DMA("gpsimd", rsb,
                         rs_o[c][dtl][0:128, tg * 512:(tg + 1) * 512],
                         waits=firstw + [fn_free[slot]])
                l2 = DMA("gpsimd", xtb,
                         xT_s[dtl * 128:(dtl + 1) * 128, c * 2048 + tg * 512:c * 2048 + (tg + 1) * 512])
                yd = OP("vector", lambda v, _r=rsb, _x=xtb, _y=yb, bi=bo_sb[:, dtl:dtl + 1]:
                        v.scalar_tensor_tensor(_y, _r, bi, _x, A.add, A.add),
                        incs="dve", waits=[l1, l2])
                stv = DMA("gpsimd",
                          yT_s[dtl * 128:(dtl + 1) * 128, c * 2048 + tg * 512:c * 2048 + (tg + 1) * 512],
                          yb, waits=[yd])
                fn_free[slot] = yd
            return stv

        for pi in range(4):
            dec_slot()
            emit_A1(pi)
        dec_slot()
        a1_all = cnt["dve"]
        a1_st = qmark("scalar")
        emit_xnt_load(0)
        emit_B2(0)
        emit_B3(0)
        avd = emit_B4(0, 0, Z)
        avd = emit_B4(0, 1, avd)
        emit_B5(0, avd, dec_per_j=1)
        emit_B6(0)
        for c in range(1, 4):
            emit_xnt_load(c)
            emit_B2(c)
            emit_B3(c)
            avd = emit_B4(c, 0, avd)
            avd = emit_B4(c, 1, avd)
            emit_B5(c, avd)
            emit_B6(c)
            emit_F(c - 1)
        last_st = emit_F(3)
        Wl("gpsimd", [last_st])
        Wl("gpsimd", qmark("gpsimd"))

        with nc.Block() as block:
            @block.sync
            def _(e):
                for f in ops["sync"]:
                    f(e)

            @block.vector
            def _(e):
                for f in ops["vector"]:
                    f(e)

            @block.scalar
            def _(e):
                for f in ops["scalar"]:
                    f(e)

            @block.tensor
            def _(e):
                for f in ops["tensor"]:
                    f(e)

            @block.gpsimd
            def _(e):
                for f in ops["gpsimd"]:
                    f(e)
    return nc


_CACHE = {}


def kernel(L0_attn_x, L0_attn_wq_codes, L0_attn_wq_scales, L0_attn_bq,
           L0_attn_wk_codes, L0_attn_wk_scales, L0_attn_bk,
           L0_attn_wv_codes, L0_attn_wv_scales, L0_attn_bv,
           L0_attn_wo_codes, L0_attn_wo_scales, L0_attn_bo,
           L0_ffn_gate_W_packed, L0_ffn_gate_scales,
           L0_ffn_up_W_packed, L0_ffn_up_scales,
           L0_ffn_down_W_packed, L0_ffn_down_scales):
    if "nc" not in _CACHE:
        _CACHE["nc"] = build_nc()
    nc = _CACHE["nc"]

    ac = np.ascontiguousarray
    xf = np.asarray(L0_attn_x, dtype=np.float32).reshape(T, D)
    xT = ac(xf.T)
    rep = np.repeat(np.arange(64), 32)
    wq_se_f = np.asarray(L0_attn_wq_scales)[rep, :]
    wk_se_f = np.asarray(L0_attn_wk_scales)[rep, :]
    wv_se_f = np.asarray(L0_attn_wv_scales)[rep, :]
    wo_se_f = np.asarray(L0_attn_wo_scales)[rep, :]
    ident = np.eye(128, dtype=ml_dtypes.bfloat16)
    bq = np.asarray(L0_attn_bq)
    bk = np.asarray(L0_attn_bk)
    bv = np.asarray(L0_attn_bv)
    bo = np.asarray(L0_attn_bo)

    in_maps = []
    for c in range(R):
        f0 = FT * c
        n0 = 2048 * c
        m = {
            "x_tok": ac(xf[TKC * c:TKC * (c + 1)]),
            "xT_s": ac(xT[f0:f0 + FT]),
            "wq_c": ac(np.asarray(L0_attn_wq_codes)[:, f0:f0 + FT]),
            "wk_c": ac(np.asarray(L0_attn_wk_codes)[:, f0:f0 + FT]),
            "wv_c": ac(np.asarray(L0_attn_wv_codes)[:, f0:f0 + FT]),
            "wq_se": ac(wq_se_f[:, f0:f0 + FT]),
            "wk_se": ac(wk_se_f[:, f0:f0 + FT]),
            "wv_se": ac(wv_se_f[:, f0:f0 + FT]),
            "wo_c": ac(np.asarray(L0_attn_wo_codes)[f0:f0 + FT, :]),
            "wo_se": ac(wo_se_f[f0:f0 + FT, :]),
            "bqkv": ac(np.stack([bq[f0:f0 + 128], bq[f0 + 128:f0 + 256],
                                 bk[f0:f0 + 128], bk[f0 + 128:f0 + 256],
                                 bv[f0:f0 + 128], bv[f0 + 128:f0 + 256]], axis=1)),
            "bo_s": ac(np.stack([bo[f0:f0 + 128], bo[f0 + 128:f0 + 256]], axis=1)),
            "g_p": ac(np.asarray(L0_ffn_gate_W_packed)[n0:n0 + 2048]),
            "g_s": ac(np.asarray(L0_ffn_gate_scales)[n0:n0 + 2048]),
            "u_p": ac(np.asarray(L0_ffn_up_W_packed)[n0:n0 + 2048]),
            "u_s": ac(np.asarray(L0_ffn_up_scales)[n0:n0 + 2048]),
            "d_p": ac(np.asarray(L0_ffn_down_W_packed)[:, 1024 * c:1024 * (c + 1)]),
            "d_s": ac(np.asarray(L0_ffn_down_scales)[:, 64 * c:64 * (c + 1)]),
            "ident": ident,
        }
        in_maps.append(m)

    res = run_bass_kernel_spmd(nc, in_maps, core_ids=list(range(R)))
    shards = [np.asarray(res.results[c]["yT_s"]) for c in range(R)]
    yT = np.concatenate(shards, axis=0)
    return ac(yT.T).reshape(B_, S, D)


# version 26
# speedup vs baseline: 1.1209x; 1.0018x over previous
import numpy as np
import ml_dtypes
from contextlib import ExitStack
import concourse.bass as bass
import concourse.mybir as mybir
from concourse.bass_utils import run_bass_kernel_spmd

F32 = mybir.dt.float32
BF16 = mybir.dt.bfloat16
I32 = mybir.dt.int32
U16 = mybir.dt.uint16
A = mybir.AluOpType
AF = mybir.ActivationFunctionType
AX = mybir.AxisListType

R = 8
B_, S, D = 4, 2048, 2048
T = B_ * S
TKC = T // R
FT = 256
ISQ = 0.08838834764831845
NQS = 4


def build_nc():
    nc = bass.Bass("TRN2", target_bir_lowering=False)

    def din(name, shape, dt):
        return nc.dram_tensor(name, shape, dt, kind="ExternalInput").ap()

    def dint(name, shape, dt):
        return nc.dram_tensor(name, shape, dt, kind="Internal").ap()

    x_tok = din("x_tok", [TKC, D], F32)
    xT_s = din("xT_s", [FT, T], F32)
    w_qkv_c = [din(f"w{p}_c", [D, FT], F32) for p in "qkv"]
    w_qkv_se = [din(f"w{p}_se", [D, FT], F32) for p in "qkv"]
    wo_c = din("wo_c", [FT, D], F32)
    wo_se = din("wo_se", [FT, D], F32)
    bqkv_h = din("bqkv", [128, 6], F32)
    bo_h = din("bo_s", [128, 2], F32)
    ffn_p = {w: din(f"{w}_p", [2048, 1024], I32) for w in ("g", "u", "d")}
    ffn_s = {w: din(f"{w}_s", [2048, 64], F32) for w in ("g", "u", "d")}
    ident_h = din("ident", [128, 128], BF16)
    yT_s = nc.dram_tensor("yT_s", [FT, T], F32, kind="ExternalOutput").ap()

    ag_in = [dint(f"ag_in{h}", [D, 512], BF16) for h in range(2)]
    ag_out = [dint(f"ag_out{h}", [R * D, 512], BF16) for h in range(2)]
    dec = {w: dint(f"dec_{w}", [16, 128, 2048], BF16) for w in ("g", "u", "d")}
    dec_a = dint("dec_a", [8, 128, 2048], BF16)
    part = [[dint(f"part{c}h{h}", [D // 2, S], F32) for h in range(2)] for c in range(4)]
    rs_o = [[dint(f"rs_o{c}h{h}", [128, S], F32) for h in range(2)] for c in range(4)]

    with ExitStack() as ctx:
        def sbuf(name, shape, dt):
            return ctx.enter_context(nc.sbuf_tensor(name, shape, dt))

        def psum(name, shape, dt):
            return ctx.enter_context(nc.psum_tensor(name, shape, dt))

        xnt = sbuf("xnt", [128, 32768], BF16)
        uni = sbuf("uni", [128, 32768], BF16)
        ao = sbuf("ao", [128, 4096], BF16)
        wpool = sbuf("wpool", [128, 8192], BF16)
        apb = sbuf("apb", [128, 4096], BF16)
        pk = sbuf("pk", [128, 2048], I32)
        dtmp = sbuf("dtmp", [128, 4096], I32)
        codes = sbuf("codes", [128, 2048], U16)
        dq = sbuf("dq", [128, 2048], BF16)
        stage = sbuf("stage", [128, 4096], BF16)
        scb = sbuf("scb", [128, 128], F32)
        id_sb = sbuf("id_sb", [128, 128], BF16)
        bqkv_sb = sbuf("bqkv_sb", [128, 6], F32)
        bo_sb = sbuf("bo_sb", [128, 2], F32)
        gg = sbuf("gg", [128, 1024], BF16)
        yp = sbuf("yp", [128, 1024], F32)
        smal = sbuf("smal", [128, 24], F32)
        xn_t = None

        P1 = psum("P1", [128, 2048], F32)
        P2 = [psum("P2a", [128, 512], F32), psum("P2b", [128, 512], F32)]
        P3 = [psum("P3a", [128, 1024], BF16), psum("P3b", [128, 1024], BF16)]

        qsems = {q: [f"{q}{i}" for i in range(NQS)] for q in ("sync", "scalar", "gpsimd")}
        semnames = ["cc", "pe", "dve", "act"] + sum(qsems.values(), [])
        sem = {n: ctx.enter_context(nc.semaphore(n)) for n in semnames}
        cnt = {n: 0 for n in semnames}
        qrot = {q: 0 for q in qsems}
        ops = {"sync": [], "vector": [], "scalar": [], "tensor": [], "gpsimd": []}

        def em(eng, fn):
            ops[eng].append(fn)

        def W(eng, s, v):
            if v > 0:
                em(eng, lambda e, _s=sem[s], _v=v: e.wait_ge(_s, _v))

        def Wl(eng, toks):
            for (s, v) in toks:
                W(eng, s, v)

        def inc(s, n=1):
            cnt[s] += n
            return cnt[s]

        def DMA(q, out_ap, in_ap, waits=()):
            Wl(q, waits)
            sname = qsems[q][qrot[q] % NQS]
            qrot[q] += 1
            c = inc(sname, 16)
            em(q, lambda e, o=out_ap, i=in_ap, sm=sem[sname]:
               e.dma_start(out=o, in_=i).then_inc(sm, 16))
            return (sname, c)

        def qmark(q):
            return [(s, cnt[s]) for s in qsems[q] if cnt[s] > 0]

        def OP(eng, build, incs=None, waits=()):
            Wl(eng, waits)
            if incs:
                c = inc(incs)
                em(eng, lambda e, b=build, sm=sem[incs]: b(e).then_inc(sm, 1))
                return (incs, c)
            em(eng, lambda e, b=build: b(e))
            return None

        u_f32 = uni[:].bitcast(F32)
        wp_f32 = wpool[:].bitcast(F32)
        jnk = codes[:].bitcast(BF16)
        fnb = dtmp[:].bitcast(F32)

        xn_t = stage[:, 0:2048]
        U_QT, U_KT = 0, 4096
        U_P = [8192, 12288]
        U_PN = [10240, 14336]
        U_VT = 16384
        U_PT = [16384, 24576]
        Z = ("pe", 0)

        p1f = [[Z], [Z], [Z], [Z]]
        p2f = [Z, Z]
        p3f = [Z, Z]

        for o, i in ((id_sb, ident_h), (bqkv_sb, bqkv_h), (bo_sb, bo_h)):
            DMA("sync", o[:], i)
        con_ld = qmark("sync")

        OP("vector", lambda v: v.memset(smal[:, 10:11], 1e-5))
        OP("vector", lambda v: v.memset(smal[:, 11:12], 16128.0), incs="dve")
        ln_done = []
        ag_done = [None, None]
        pe_tr = Z
        for tt in range(8):
            slot = tt % 2
            xsl = wp_f32[:, slot * 2048:(slot + 1) * 2048]
            ldw = [ln_done[tt - 2]] if tt >= 2 else ()
            ldv = DMA("sync", xsl, x_tok[tt * 128:(tt + 1) * 128, :], waits=ldw)
            rs1 = OP("vector", lambda v, x=xsl: v.reduce_sum(smal[:, 0:1], x, axis=AX.X),
                     incs="dve", waits=[ldv])
            nmn = OP("vector", lambda v: v.tensor_single_scalar(smal[:, 2:3], smal[:, 0:1],
                     -1.0 / D, A.mult), incs="dve", waits=[rs1])
            sq = OP("scalar", lambda s, x=xsl: s.activation(out=jnk[:], in_=x, func=AF.Square,
                    accum_out=smal[:, 1:2]), incs="act", waits=[ldv])
            OP("vector", lambda v: v.tensor_single_scalar(smal[:, 3:4], smal[:, 1:2],
                                                          1.0 / D, A.mult), waits=[sq, nmn])
            m2c = OP("vector", lambda v: v.tensor_tensor(smal[:, 4:5], smal[:, 2:3],
                     smal[:, 2:3], A.mult), incs="dve")
            vd = OP("vector", lambda v: v.tensor_tensor(smal[:, 12:13], smal[:, 3:4],
                    smal[:, 4:5], A.subtract), incs="dve", waits=[m2c])
            sdc = OP("scalar", lambda s: s.activation(out=smal[:, 13:14], in_=smal[:, 12:13],
                     func=AF.Sqrt, bias=smal[:, 10:11], scale=1.0), incs="act", waits=[vd])
            rcc = OP("vector", lambda v: v.reciprocal(smal[:, 5:6], smal[:, 13:14]),
                     incs="dve", waits=[sdc])
            xw = [rcc] + ([pe_tr] if tt >= 1 else [])
            xnd = OP("vector", lambda v, x=xsl: v.tensor_scalar(xn_t[:], x, smal[:, 2:3],
                     smal[:, 5:6], A.add, A.mult), incs="dve", waits=xw)
            ln_done.append(xnd)
            Wl("tensor", [xnd])
            if tt == 0:
                Wl("tensor", con_ld)
            for half in range(2):
                pd = P3[half]
                Wl("tensor", [p3f[half]])
                for k in range(8):
                    dtl = half * 8 + k
                    pe_tr = OP("tensor", lambda t, p=pd[:, k * 128:(k + 1) * 128],
                               i=xn_t[:, dtl * 128:(dtl + 1) * 128]:
                               t.transpose(p, i, id_sb[:]),
                               incs="pe" if k == 7 else None) or pe_tr
                dst = uni[:].rearrange("p (dt q) -> p dt q", q=1024)[
                    :, half * 8:half * 8 + 8, tt * 128:(tt + 1) * 128]
                ev = OP("scalar", lambda s, o=dst,
                        i=pd[:].rearrange("p (dt q) -> p dt q", q=128):
                        s.copy(out=o, in_=i), incs="act", waits=[pe_tr])
                p3f[half] = ev
            if tt in (3, 7):
                th = tt // 4
                W("gpsimd", "act", cnt["act"])
                for dtl in range(16):
                    DMA("gpsimd", ag_in[th][dtl * 128:(dtl + 1) * 128, :],
                        uni[:, dtl * 1024 + th * 512:dtl * 1024 + th * 512 + 512])
                Wl("gpsimd", qmark("gpsimd"))
                ag_done[th] = OP("gpsimd", lambda g, _h=th: g.collective_compute(
                    "AllGather", A.bypass, replica_groups=[list(range(R))],
                    ins=[ag_in[_h].opt()], outs=[ag_out[_h].opt()]), incs="cc")
        ag_st = qmark("gpsimd")

        st_free2 = [("act", 0), ("act", 0), ("act", 0), ("act", 0)]
        a1_srcs = [(w_qkv_c[0], w_qkv_se[0]), (w_qkv_c[1], w_qkv_se[1]),
                   (w_qkv_c[2], w_qkv_se[2]), (wo_c, wo_se)]
        a1_ttp = [None] * 4
        deca_st = {}
        a1_state = {"sidx": 0}

        def emit_A1(pi):
            sidx = a1_state["sidx"]
            ch, sh = a1_srcs[pi]
            base = (pi % 2) * 8192
            lw = list(ag_st) if pi < 2 else [a1_ttp[pi - 2]]
            toks = []
            if pi < 3:
                cin = ch[:].rearrange("(dt p) f -> p dt f", p=128)
                sin = sh[:].rearrange("(dt p) f -> p dt f", p=128)
                toks.append(DMA("scalar",
                                u_f32[:, base:base + 4096].rearrange("p (dt f) -> p dt f", f=FT),
                                cin, waits=lw))
                toks.append(DMA("scalar",
                                u_f32[:, base + 4096:base + 8192].rearrange("p (dt f) -> p dt f", f=FT),
                                sin))
            else:
                toks.append(DMA("scalar", u_f32[:, base:base + 2048], ch[0:128, :], waits=lw))
                toks.append(DMA("scalar", u_f32[:, base + 2048:base + 4096], ch[128:256, :]))
                toks.append(DMA("scalar", u_f32[:, base + 4096:base + 6144], sh[0:128, :]))
                toks.append(DMA("scalar", u_f32[:, base + 6144:base + 8192], sh[128:256, :]))
            Wl("vector", toks)
            a1_tt = None
            for f in range(2):
                for hg in range(2):
                    ss = sidx % 4
                    sidx += 1
                    Wl("vector", [st_free2[ss]])
                    for k in range(8):
                        dtl = hg * 8 + k
                        if pi < 3:
                            a_ = u_f32[:, base + dtl * 256 + f * 128:base + dtl * 256 + f * 128 + 128]
                            b_ = u_f32[:, base + 4096 + dtl * 256 + f * 128:base + 4096 + dtl * 256 + f * 128 + 128]
                        else:
                            a_ = u_f32[:, base + f * 2048 + dtl * 128:base + f * 2048 + (dtl + 1) * 128]
                            b_ = u_f32[:, base + 4096 + f * 2048 + dtl * 128:base + 4096 + f * 2048 + (dtl + 1) * 128]
                        o = stage[:, ss * 1024 + k * 128:ss * 1024 + (k + 1) * 128]
                        a1_tt = OP("vector", lambda v, _o=o, _a=a_, _b=b_:
                                   v.tensor_tensor(_o, _a, _b, A.mult),
                                   incs="dve" if k == 7 else None) or a1_tt
                    stv = DMA("scalar", dec_a[2 * pi + f][:, hg * 1024:(hg + 1) * 1024],
                              stage[:, ss * 1024:(ss + 1) * 1024], waits=[a1_tt])
                    st_free2[ss] = stv
                    deca_st.setdefault(2 * pi + f, []).append(stv)
            a1_ttp[pi] = a1_tt
            a1_state["sidx"] = sidx

        Tt = [dtmp[:, i * 1024:(i + 1) * 1024] for i in range(4)]
        pk_free = [("dve", 0), ("dve", 0)]
        sc_free = [("dve", 0), ("dve", 0)]
        st_free = st_free2
        dec_store = {}
        dec_ptr = {}
        dec_i = [0]
        dec_pend = []
        dec_order = [(w, rt) for rt in range(16) for w in ("g", "u")] + \
                    [("d", rt) for rt in range(16)]
        p0_act = cnt["act"]

        def emit_decode_dve(n):
            for _ in range(n):
                if dec_i[0] >= len(dec_order):
                    return
                i = dec_i[0]
                dec_i[0] += 1
                w, rt = dec_order[i]
                slot = i % 2
                pks = pk[:, slot * 1024:(slot + 1) * 1024]
                scs = scb[:, slot * 64:(slot + 1) * 64]
                pkv = DMA("gpsimd", pks, ffn_p[w][rt * 128:(rt + 1) * 128, :],
                          waits=[pk_free[slot]])
                scv = DMA("gpsimd", scs, ffn_s[w][rt * 128:(rt + 1) * 128, :],
                          waits=[sc_free[slot]])
                w0 = [pkv] + ([("act", p0_act)] if i == 0 else [])
                dcd = None
                for lohi in range(2):
                    if lohi == 0:
                        OP("vector", lambda v, p=pks: v.tensor_single_scalar(
                           Tt[0][:], p, 7, A.bitwise_and), waits=w0)
                        OP("vector", lambda v, p=pks: v.tensor_single_scalar(
                           Tt[1][:], p, 8, A.bitwise_and))
                    else:
                        OP("vector", lambda v, p=pks: v.tensor_scalar(
                           Tt[0][:], p, 4, 7, A.logical_shift_right, A.bitwise_and))
                        pkfree = OP("vector", lambda v, p=pks: v.tensor_scalar(
                                    Tt[1][:], p, 4, 8, A.logical_shift_right, A.bitwise_and),
                                    incs="dve")
                        pk_free[slot] = pkfree
                    bbv = OP("vector", lambda v: v.tensor_scalar(Tt[2][:], Tt[0][:],
                             64, 16128, A.mult, A.add), incs="dve")
                    mmn = OP("vector", lambda v: v.scalar_tensor_tensor(Tt[3][:], Tt[0][:],
                             16128.0, Tt[2][:], A.mult, A.min), incs="dve", waits=[bbv])
                    dst = codes[:, lohi:2048:2]
                    dcd = OP("vector", lambda v, _d=dst: v.scalar_tensor_tensor(
                             _d, Tt[1][:], 4096.0, Tt[3][:], A.mult, A.add),
                             incs="dve", waits=[mmn])
                dqd = OP("vector", lambda v, _s=scs:
                         v.tensor_tensor(dq[:].rearrange("p (j r) -> p j r", r=32),
                                         codes[:].bitcast(BF16).rearrange("p (j r) -> p j r", r=32),
                                         _s.unsqueeze(2).broadcast_to([128, 64, 32]),
                                         A.mult), incs="dve",
                         waits=[scv, dcd, dec_ptr.get(i - 1, Z)])
                sc_free[slot] = dqd
                dec_pend.append((i, w, rt, dqd))

        def emit_decode_pe():
            i, w, rt, dqd = dec_pend.pop(0)
            Wl("tensor", [dqd])
            ptr = None
            for half in range(2):
                pd = P3[half]
                Wl("tensor", [p3f[half]])
                for k in range(8):
                    kt2 = half * 8 + k
                    ptr = OP("tensor", lambda t, p=pd[:, k * 128:(k + 1) * 128],
                             i2=dq[:, kt2 * 128:(kt2 + 1) * 128]:
                             t.transpose(p, i2, id_sb[:]),
                             incs="pe" if k == 7 else None) or ptr
                ss2 = (2 * i + half) % 4
                ev = OP("scalar", lambda s, o=stage[:, ss2 * 1024:(ss2 + 1) * 1024],
                        i2=pd[:]: s.copy(out=o, in_=i2), incs="act",
                        waits=[ptr, st_free[ss2]])
                p3f[half] = ev
                stv = DMA("scalar", dec[w][rt][:, half * 1024:(half + 1) * 1024],
                          stage[:, ss2 * 1024:(ss2 + 1) * 1024], waits=[ev])
                st_free[ss2] = stv
            dec_ptr[i] = ptr
            dec_store[(w, rt)] = stv

        def emit_decode(n):
            for _ in range(n):
                if dec_i[0] >= len(dec_order):
                    return
                emit_decode_dve(1)
                emit_decode_pe()

        def dec_slot():
            if dec_pend:
                emit_decode_pe()
            if dec_i[0] < len(dec_order):
                emit_decode_dve(1)

        xnt_h0 = [None] * 4
        xnt_all = [None] * 4
        apb_use = [Z, Z]
        wslot_use = [Z, Z, Z, Z]
        yp_free = [("act", 0), ("act", 0)]
        b5_last = [0, 0, 0, 0]
        rs_done = [0, 0, 0, 0]
        fn_free = [("dve", 0), ("dve", 0)]
        gg_free = [("dve", 0), ("dve", 0)]
        qk_ev = [Z]
        ao_ev = [Z]
        b5_intd = [0]

        def emit_xnt_load(c):
            w0 = [("pe", b5_last[c - 1])] if c > 0 else []
            for th in range(2):
                waits = w0 + ([("cc", ag_done[th][1])] if c == 0 else [])
                w0 = []
                for dtl in range(16):
                    for h2 in range(2):
                        DMA("sync",
                            xnt[:, dtl * 2048 + h2 * 1024 + th * 512:
                                dtl * 2048 + h2 * 1024 + th * 512 + 512],
                            ag_out[th][(2 * c + h2) * D + dtl * 128:
                                       (2 * c + h2) * D + (dtl + 1) * 128, :],
                            waits=waits)
                        waits = ()
                if th == 0:
                    xnt_h0[c] = qmark("sync")
            xnt_all[c] = qmark("sync")

        def emit_B2(c):
            last_ev = Z
            for ft in range(6):
                slot = ft % 2
                apb_ld = DMA("sync", apb[:, slot * 2048:(slot + 1) * 2048],
                             dec_a[ft], waits=deca_st[ft] + [apb_use[slot]])
                for tg in (0, 2, 1, 3):
                    g = (4 * ft + tg) % 2
                    pp = P2[g]
                    Wl("tensor", [apb_ld] + (xnt_h0[c] if tg in (0, 2) else xnt_all[c]))
                    Wl("tensor", [p2f[g]])
                    cp = None
                    for dtl in range(16):
                        cp = OP("tensor", lambda t, _p=pp[:], _w=apb[:, slot * 2048 + dtl * 128:
                                slot * 2048 + (dtl + 1) * 128],
                                _x=xnt[:, dtl * 2048 + tg * 512:dtl * 2048 + (tg + 1) * 512],
                                st=(dtl == 0), sp=(dtl == 15): t.matmul(_p, _w, _x, start=st, stop=sp),
                                incs="pe" if dtl == 15 else None) or cp
                    if tg == 3:
                        apb_use[slot] = cp
                    if ft < 2:
                        dst = uni[:, U_QT + ft * 2048 + tg * 512:U_QT + ft * 2048 + (tg + 1) * 512]
                    elif ft < 4:
                        dst = uni[:, U_KT + (ft - 2) * 2048 + tg * 512:U_KT + (ft - 2) * 2048 + (tg + 1) * 512]
                    else:
                        dst = uni[:, U_VT + (ft - 4) * 2048 + tg * 512:U_VT + (ft - 4) * 2048 + (tg + 1) * 512]
                    ev = OP("scalar", lambda s, o=dst, i2=pp[:], bi=bqkv_sb[:, ft:ft + 1]:
                            s.activation(out=o, in_=i2, func=AF.Identity, bias=bi),
                            incs="act", waits=[cp])
                    p2f[g] = ev
                    last_ev = ev
                    if c == 0:
                        dec_slot()
            qk_ev[0] = last_ev

        def emit_B3(c):
            Wl("tensor", [qk_ev[0]])
            for h in range(2):
                for half in range(2):
                    pd = P3[half]
                    Wl("tensor", [p3f[half]])
                    ptr = None
                    for k in range(8):
                        tk = half * 8 + k
                        ptr = OP("tensor", lambda t, p=pd[:, k * 128:(k + 1) * 128],
                                 i2=uni[:, U_VT + h * 2048 + tk * 128:U_VT + h * 2048 + (tk + 1) * 128]:
                                 t.transpose(p, i2, id_sb[:]),
                                 incs="pe" if k == 7 else None) or ptr
                    dstv = apb[:, half * 2048:(half + 1) * 2048].rearrange(
                        "p (a r) -> p a r", r=256)[:, :, h * 128:(h + 1) * 128]
                    ev = OP("scalar", lambda s, o=dstv,
                            i2=pd[:].rearrange("p (a r) -> p a r", r=128):
                            s.copy(out=o, in_=i2), incs="act", waits=[ptr])
                    p3f[half] = ev
                    if c == 0:
                        dec_slot()

        def emit_B4(c, avd):
            prev_ea = [Z, Z]
            prev_eb = [Z, Z]
            prev_nm = [Z, Z]
            ptev = [[Z, Z], [Z, Z]]
            sb = [6, 16]
            for q in range(16):
                for h in range(2):
                    if q == 0 and h == 0:
                        Wl("tensor", [qk_ev[0]])
                        for k in range(4):
                            Wl("tensor", p1f[k])
                    b = sb[h]
                    sc = [None, None]
                    for kt in range(4):
                        if (q > 0 or h > 0) and kt in (0, 2):
                            Wl("tensor", [prev_ea[1 - h] if kt == 0 else prev_eb[1 - h]] +
                               ([prev_nm[1 - h]] if kt == 0 else []))
                        r_ = OP("tensor", lambda t, _p=P1[:, kt * 512:(kt + 1) * 512],
                                _q=uni[:, U_QT + h * 2048 + q * 128:U_QT + h * 2048 + (q + 1) * 128],
                                _k=uni[:, U_KT + h * 2048 + kt * 512:U_KT + h * 2048 + (kt + 1) * 512]:
                                t.matmul(_p, _q, _k, start=True, stop=True),
                                incs="pe" if kt in (1, 3) else None)
                        if kt == 1:
                            sc[0] = r_
                        elif kt == 3:
                            sc[1] = r_
                    rmx = OP("vector", lambda v, _b=b: v.reduce_max(smal[:, _b:_b + 1],
                             P1[:, 0:512], axis=AX.X, negate=True), incs="dve",
                             waits=[sc[0], prev_ea[1 - h]])
                    nm = OP("vector", lambda v, _b=b: v.tensor_single_scalar(
                            smal[:, _b + 1:_b + 2], smal[:, _b:_b + 1], ISQ, A.mult),
                            incs="dve", waits=[rmx])
                    ea = OP("scalar", lambda s, _b=b, _h=h: s.activation(
                            out=uni[:, U_P[_h]:U_P[_h] + 1024],
                            in_=P1[:, 0:1024], func=AF.Exp, scale=ISQ,
                            bias=smal[:, _b + 1:_b + 2],
                            accum_out=smal[:, _b + 2:_b + 3]), incs="act", waits=[sc[0], nm])
                    eb = OP("scalar", lambda s, _b=b, _h=h: s.activation(
                            out=uni[:, U_P[_h] + 1024:U_P[_h] + 2048],
                            in_=P1[:, 1024:2048], func=AF.Exp, scale=ISQ,
                            bias=smal[:, _b + 1:_b + 2],
                            accum_out=smal[:, _b + 4:_b + 5]), incs="act", waits=[sc[1]])
                    sad = OP("vector", lambda v, _b=b: v.tensor_tensor(
                             smal[:, _b + 3:_b + 4], smal[:, _b + 2:_b + 3],
                             smal[:, _b + 4:_b + 5], A.add), incs="dve", waits=[eb])
                    rcp = OP("vector", lambda v, _b=b: v.reciprocal(
                             smal[:, _b + 3:_b + 4], smal[:, _b + 3:_b + 4]),
                             incs="dve", waits=[sad])
                    pn = OP("vector", lambda v, _b=b, _h=h: v.tensor_single_scalar(
                            uni[:, U_PN[_h]:U_PN[_h] + 2048],
                            uni[:, U_P[_h]:U_P[_h] + 2048],
                            smal[:, _b + 3:_b + 4], A.mult), incs="dve", waits=[rcp])
                    prev_ea[h], prev_eb[h], prev_nm[h] = ea, eb, nm
                    ql = q % 4
                    Wl("tensor", [pn])
                    for half in range(2):
                        pd = P3[half]
                        Wl("tensor", [p3f[half]])
                        ptr = None
                        for k in range(8):
                            kt = half * 8 + k
                            ptr = OP("tensor", lambda t, p=pd[:, k * 128:(k + 1) * 128],
                                     i2=uni[:, U_PN[h] + kt * 128:U_PN[h] + (kt + 1) * 128]:
                                     t.transpose(p, i2, id_sb[:]),
                                     incs="pe" if k == 7 else None) or ptr
                        dstv = uni[:, U_PT[h] + half * 8 * 512:U_PT[h] + (half * 8 + 8) * 512].rearrange(
                            "p (a r) -> p a r", r=512)[:, :, ql * 128:(ql + 1) * 128]
                        if half == 0:
                            ev = OP("vector", lambda v, o=dstv,
                                    i2=pd[:].rearrange("p (a r) -> p a r", r=128):
                                    v.tensor_copy(o, i2), incs="dve",
                                    waits=[ptr, avd])
                        else:
                            ev = OP("scalar", lambda s, o=dstv,
                                    i2=pd[:].rearrange("p (a r) -> p a r", r=128):
                                    s.copy(out=o, in_=i2), incs="act", waits=[ptr])
                        p3f[half] = ev
                        ptev[h][half] = ev
                    if ql == 3:
                        g4 = q // 4
                        gs = (2 * g4 + h) % 2
                        Wl("tensor", ptev[h] + [p2f[gs]])
                        for kt in range(16):
                            r_ = OP("tensor", lambda t, _p=P2[gs][:],
                                    _v=apb[:, kt * 256 + h * 128:kt * 256 + (h + 1) * 128],
                                    _pt=uni[:, U_PT[h] + kt * 512:U_PT[h] + (kt + 1) * 512],
                                    st=(kt == 0), sp=(kt == 15): t.matmul(_p, _v, _pt, start=st, stop=sp),
                                    incs="pe" if kt == 15 else None)
                            avd = r_ or avd
                        ev = OP("scalar", lambda s, o=ao[:, h * 2048 + g4 * 512:h * 2048 + (g4 + 1) * 512],
                                i2=P2[gs][:]: s.copy(out=o, in_=i2), incs="act", waits=[avd])
                        p2f[gs] = ev
                        ao_ev[0] = ev
            p1f[0] = [prev_ea[1], prev_nm[1], prev_ea[0], prev_nm[0]]
            p1f[1] = [prev_ea[1], prev_ea[0]]
            p1f[2] = [prev_eb[1], prev_eb[0]]
            p1f[3] = [prev_eb[1], prev_eb[0]]
            apb_use[0] = avd
            apb_use[1] = avd
            return avd

        def emit_B5(c, avd, dec_per_j=0):
            first = True
            for j in range(16):
                if dec_per_j:
                    emit_decode(dec_per_j)
                ldg = DMA("sync", wpool[:, (j % 2) * 2048:(j % 2 + 1) * 2048], dec["g"][j],
                          waits=[dec_store[("g", j)], wslot_use[j % 2]])
                ldu = DMA("sync", wpool[:, (2 + j % 2) * 2048:(3 + j % 2) * 2048], dec["u"][j],
                          waits=[dec_store[("u", j)], wslot_use[2 + j % 2]])
                for tg in range(4):
                    qb = tg % 2
                    bg = P1[:, (2 * qb) * 512:(2 * qb + 1) * 512]
                    bu = P1[:, (2 * qb + 1) * 512:(2 * qb + 2) * 512]
                    Wl("tensor", [ldg, ldu] + xnt_all[c])
                    Wl("tensor", p1f[2 * qb] + p1f[2 * qb + 1])
                    gmm = umm = None
                    for kd in range(16):
                        gmm = OP("tensor", lambda t, _p=bg, _w=wpool[:, (j % 2) * 2048 + kd * 128:
                                 (j % 2) * 2048 + (kd + 1) * 128],
                                 _x=xnt[:, kd * 2048 + tg * 512:kd * 2048 + (tg + 1) * 512],
                                 st=(kd == 0), sp=(kd == 15): t.matmul(_p, _w, _x, start=st, stop=sp),
                                 incs="pe" if kd == 15 else None) or gmm
                    for kd in range(16):
                        umm = OP("tensor", lambda t, _p=bu, _w=wpool[:, (2 + j % 2) * 2048 + kd * 128:
                                 (2 + j % 2) * 2048 + (kd + 1) * 128],
                                 _x=xnt[:, kd * 2048 + tg * 512:kd * 2048 + (tg + 1) * 512],
                                 st=(kd == 0), sp=(kd == 15): t.matmul(_p, _w, _x, start=st, stop=sp),
                                 incs="pe" if kd == 15 else None) or umm
                    if tg == 3:
                        wslot_use[j % 2] = gmm
                        wslot_use[2 + j % 2] = umm
                    geld = OP("scalar", lambda s, _g=bg, o=gg[:, qb * 512:(qb + 1) * 512]:
                              s.activation(out=o, in_=_g, func=AF.Gelu_apprx_tanh),
                              incs="act", waits=[gmm, gg_free[qb]])
                    p1f[2 * qb] = [geld]
                    iw = [umm, geld]
                    if first:
                        iw += [avd, ao_ev[0]]
                        first = False
                    intd = OP("vector", lambda v, _u=bu,
                              o=uni[:, j * 2048 + tg * 512:j * 2048 + (tg + 1) * 512],
                              g2=gg[:, qb * 512:(qb + 1) * 512]:
                              v.tensor_tensor(o, g2, _u, A.mult), incs="dve", waits=iw)
                    p1f[2 * qb + 1] = [intd]
                    gg_free[qb] = intd
            while dec_pend:
                emit_decode_pe()
            while dec_i[0] < len(dec_order) and dec_per_j:
                emit_decode_dve(1)
                emit_decode_pe()
            b5_last[c] = cnt["pe"]
            b5_intd[0] = cnt["dve"]

        def emit_B6(c):
            W("tensor", "act", ao_ev[0][1])
            W("tensor", "dve", b5_intd[0])
            if c == 0:
                W("tensor", "dve", a1_all)
            wo_toks = []
            for kt in range(2):
                wo_toks.append(DMA("gpsimd", wpool[:, (2 + kt) * 2048:(3 + kt) * 2048],
                                   dec_a[6 + kt], waits=deca_st[6 + kt] + [wslot_use[2 + kt]]))
            part_last = None
            rs_half = [None, None]
            for idx, dtl in enumerate(list(range(0, 16, 2)) + list(range(1, 16, 2))):
                slot = idx % 2
                ldd = DMA("gpsimd", wpool[:, slot * 2048:(slot + 1) * 2048],
                          dec["d"][dtl],
                          waits=[dec_store[("d", dtl)], wslot_use[slot]])
                for tg in range(4):
                    gs = (4 * dtl + tg) % 2
                    pp = P2[gs]
                    Wl("tensor", [ldd] + wo_toks)
                    Wl("tensor", [p2f[gs]])
                    for kt in range(2):
                        OP("tensor", lambda t, _p=pp[:],
                           _w=wpool[:, (2 + kt) * 2048 + dtl * 128:(2 + kt) * 2048 + (dtl + 1) * 128],
                           _a=ao[:, kt * 2048 + tg * 512:kt * 2048 + (tg + 1) * 512],
                           st=(kt == 0): t.matmul(_p, _w, _a, start=st, stop=False))
                    pdn = None
                    for kn in range(16):
                        pdn = OP("tensor", lambda t, _p=pp[:],
                                 _w=wpool[:, slot * 2048 + kn * 128:slot * 2048 + (kn + 1) * 128],
                                 _i=uni[:, kn * 2048 + tg * 512:kn * 2048 + (tg + 1) * 512],
                                 sp=(kn == 15): t.matmul(_p, _w, _i, start=False, stop=sp),
                                 incs="pe" if kn == 15 else None) or pdn
                    if tg == 3:
                        wslot_use[slot] = pdn
                        if dtl == 15:
                            wslot_use[2] = pdn
                            wslot_use[3] = pdn
                    ypc = OP("scalar", lambda s, o=yp[:, gs * 512:(gs + 1) * 512], i2=pp[:]:
                             s.copy(out=o, in_=i2), incs="act",
                             waits=[pdn, yp_free[gs]])
                    p2f[gs] = ypc
                    part_last = DMA("scalar",
                                    part[c][dtl % 2][(dtl // 2) * 128:(dtl // 2 + 1) * 128,
                                                     tg * 512:(tg + 1) * 512],
                                    yp[:, gs * 512:(gs + 1) * 512], waits=[ypc])
                    yp_free[gs] = part_last
                if idx in (7, 15):
                    h = idx // 8
                    Wl("gpsimd", qmark("scalar"))
                    rs_half[h] = OP("gpsimd", lambda g, _c=c, _h=h: g.collective_compute(
                        "ReduceScatter", A.add, replica_groups=[list(range(R))],
                        ins=[part[_c][_h].opt()], outs=[rs_o[_c][_h].opt()]), incs="cc")
            rs_done[c] = rs_half

        def emit_F(c):
            seen = set()
            stv = None
            for pc in range(8):
                dtl, tg = pc // 4, pc % 4
                slot = pc % 2
                rsb = fnb[:, slot * 1536:slot * 1536 + 512]
                xtb = fnb[:, slot * 1536 + 512:slot * 1536 + 1024]
                yb = fnb[:, slot * 1536 + 1024:slot * 1536 + 1536]
                firstw = [("cc", rs_done[c][dtl][1])] if dtl not in seen else []
                seen.add(dtl)
                l1 = DMA("gpsimd", rsb,
                         rs_o[c][dtl][0:128, tg * 512:(tg + 1) * 512],
                         waits=firstw + [fn_free[slot]])
                l2 = DMA("gpsimd", xtb,
                         xT_s[dtl * 128:(dtl + 1) * 128, c * 2048 + tg * 512:c * 2048 + (tg + 1) * 512])
                yd = OP("vector", lambda v, _r=rsb, _x=xtb, _y=yb, bi=bo_sb[:, dtl:dtl + 1]:
                        v.scalar_tensor_tensor(_y, _r, bi, _x, A.add, A.add),
                        incs="dve", waits=[l1, l2])
                stv = DMA("gpsimd",
                          yT_s[dtl * 128:(dtl + 1) * 128, c * 2048 + tg * 512:c * 2048 + (tg + 1) * 512],
                          yb, waits=[yd])
                fn_free[slot] = yd
            return stv

        for pi in range(4):
            dec_slot()
            emit_A1(pi)
        dec_slot()
        a1_all = cnt["dve"]
        a1_st = qmark("scalar")
        emit_xnt_load(0)
        emit_B2(0)
        emit_B3(0)
        avd = emit_B4(0, Z)
        emit_B5(0, avd, dec_per_j=1)
        emit_B6(0)
        for c in range(1, 4):
            emit_xnt_load(c)
            emit_B2(c)
            emit_B3(c)
            avd = emit_B4(c, avd)
            emit_B5(c, avd)
            emit_B6(c)
            emit_F(c - 1)
        last_st = emit_F(3)
        Wl("gpsimd", [last_st])
        Wl("gpsimd", qmark("gpsimd"))

        with nc.Block() as block:
            @block.sync
            def _(e):
                for f in ops["sync"]:
                    f(e)

            @block.vector
            def _(e):
                for f in ops["vector"]:
                    f(e)

            @block.scalar
            def _(e):
                for f in ops["scalar"]:
                    f(e)

            @block.tensor
            def _(e):
                for f in ops["tensor"]:
                    f(e)

            @block.gpsimd
            def _(e):
                for f in ops["gpsimd"]:
                    f(e)
    return nc


_CACHE = {}


def kernel(L0_attn_x, L0_attn_wq_codes, L0_attn_wq_scales, L0_attn_bq,
           L0_attn_wk_codes, L0_attn_wk_scales, L0_attn_bk,
           L0_attn_wv_codes, L0_attn_wv_scales, L0_attn_bv,
           L0_attn_wo_codes, L0_attn_wo_scales, L0_attn_bo,
           L0_ffn_gate_W_packed, L0_ffn_gate_scales,
           L0_ffn_up_W_packed, L0_ffn_up_scales,
           L0_ffn_down_W_packed, L0_ffn_down_scales):
    if "nc" not in _CACHE:
        _CACHE["nc"] = build_nc()
    nc = _CACHE["nc"]

    ac = np.ascontiguousarray
    xf = np.asarray(L0_attn_x, dtype=np.float32).reshape(T, D)
    xT = ac(xf.T)
    rep = np.repeat(np.arange(64), 32)
    wq_se_f = np.asarray(L0_attn_wq_scales)[rep, :]
    wk_se_f = np.asarray(L0_attn_wk_scales)[rep, :]
    wv_se_f = np.asarray(L0_attn_wv_scales)[rep, :]
    wo_se_f = np.asarray(L0_attn_wo_scales)[rep, :]
    ident = np.eye(128, dtype=ml_dtypes.bfloat16)
    bq = np.asarray(L0_attn_bq)
    bk = np.asarray(L0_attn_bk)
    bv = np.asarray(L0_attn_bv)
    bo = np.asarray(L0_attn_bo)

    in_maps = []
    for c in range(R):
        f0 = FT * c
        n0 = 2048 * c
        m = {
            "x_tok": ac(xf[TKC * c:TKC * (c + 1)]),
            "xT_s": ac(xT[f0:f0 + FT]),
            "wq_c": ac(np.asarray(L0_attn_wq_codes)[:, f0:f0 + FT]),
            "wk_c": ac(np.asarray(L0_attn_wk_codes)[:, f0:f0 + FT]),
            "wv_c": ac(np.asarray(L0_attn_wv_codes)[:, f0:f0 + FT]),
            "wq_se": ac(wq_se_f[:, f0:f0 + FT]),
            "wk_se": ac(wk_se_f[:, f0:f0 + FT]),
            "wv_se": ac(wv_se_f[:, f0:f0 + FT]),
            "wo_c": ac(np.asarray(L0_attn_wo_codes)[f0:f0 + FT, :]),
            "wo_se": ac(wo_se_f[f0:f0 + FT, :]),
            "bqkv": ac(np.stack([bq[f0:f0 + 128], bq[f0 + 128:f0 + 256],
                                 bk[f0:f0 + 128], bk[f0 + 128:f0 + 256],
                                 bv[f0:f0 + 128], bv[f0 + 128:f0 + 256]], axis=1)),
            "bo_s": ac(np.stack([bo[f0:f0 + 128], bo[f0 + 128:f0 + 256]], axis=1)),
            "g_p": ac(np.asarray(L0_ffn_gate_W_packed)[n0:n0 + 2048]),
            "g_s": ac(np.asarray(L0_ffn_gate_scales)[n0:n0 + 2048]),
            "u_p": ac(np.asarray(L0_ffn_up_W_packed)[n0:n0 + 2048]),
            "u_s": ac(np.asarray(L0_ffn_up_scales)[n0:n0 + 2048]),
            "d_p": ac(np.asarray(L0_ffn_down_W_packed)[:, 1024 * c:1024 * (c + 1)]),
            "d_s": ac(np.asarray(L0_ffn_down_scales)[:, 64 * c:64 * (c + 1)]),
            "ident": ident,
        }
        in_maps.append(m)

    res = run_bass_kernel_spmd(nc, in_maps, core_ids=list(range(R)))
    shards = [np.asarray(res.results[c]["yT_s"]) for c in range(R)]
    yT = np.concatenate(shards, axis=0)
    return ac(yT.T).reshape(B_, S, D)
